# version 15
# baseline (speedup 1.0000x reference)
import math
from contextlib import ExitStack

import numpy as np
import ml_dtypes

import concourse.bass as bass
import concourse.mybir as mybir
from concourse.ap import AP as RawAP
from concourse.bass_utils import run_bass_kernel_spmd

F32 = mybir.dt.float32
BF16 = mybir.dt.bfloat16
AF = mybir.ActivationFunctionType
ALU = mybir.AluOpType
AX = mybir.AxisListType

B, L, H = 2, 8192, 2048
NH, HD = 32, 64
QB, LB = 1024, 512
SPAN = QB + LB
NSC = SPAN // 128
P_PAD = (2560, 6656)
NPRE = (P_PAD[0] // 128, P_PAD[1] // 128)
NPT = (P_PAD[0] // 512, P_PAD[1] // 512)
TWO_PI = 6.283185307179586
CW_C1 = 6.28125
CW_C2 = TWO_PI - CW_C1
LN10K_32 = math.log(10000.0) / 32.0
NEGBIG = -30000.0

V_FLAT = 12 * 2080
VG_FLAT = 52 * 128


def _core_blocks(c):
    jlo, jhi = min(c, 7 - c), max(c, 7 - c)
    b0 = 0 if c <= 3 else 1
    return [(b0, jlo), (1 - b0, jhi)]



def _tile_wT(W):
    t = np.ascontiguousarray(W.T).reshape(16, 128, 16, 128)
    t = t.transpose(2, 1, 0, 3)
    t = t.reshape(8, 2, 128, 16, 128).transpose(0, 2, 1, 3, 4)
    return np.ascontiguousarray(t)


def stage_inputs(inputs):
    x = np.ascontiguousarray(inputs["L0_attn_hidden_states"], dtype=np.float32)
    WqT = _tile_wT(np.asarray(inputs["L0_attn_Wq"], np.float32))
    WkT = _tile_wT(np.asarray(inputs["L0_attn_Wk"], np.float32))
    WvT = _tile_wT(np.asarray(inputs["L0_attn_Wv"], np.float32))
    WoT = np.ascontiguousarray(np.asarray(inputs["L0_attn_Wo"], np.float32)
                               .T.reshape(16, 128, 2048))
    WfT = np.ascontiguousarray(np.asarray(inputs["L0_ffn_weight"], np.float32)
                               .T.reshape(16, 128, 2048))
    bo = np.ascontiguousarray(np.asarray(inputs["L0_attn_bo"], np.float32)
                              .reshape(1, 2048))

    prot = np.zeros((128, 128), np.float32)
    for blk in range(2):
        for m in range(64):
            perm = m + 32 if m < 32 else m - 32
            prot[blk * 64 + perm, blk * 64 + m] = 1.0
    diag = np.triu(np.ones((128, 128), np.float32))
    sgn = np.where(np.arange(128) % 64 < 32, -1.0, 1.0).astype(np.float32).reshape(128, 1)
    ident = np.eye(128, dtype=np.float32)

    in_maps = []
    for c in range(8):
        x_span = np.zeros((2, SPAN, H), np.float32)
        x_spanT = np.zeros((2, 128, 16, SPAN), np.float32)
        kvn_span = np.full((2, 128, NSC), NEGBIG, np.float32)
        posb = np.zeros((2, 64), np.float32)
        xp = [np.zeros((128, 16, P_PAD[0]), np.float32),
              np.zeros((128, 16, P_PAD[1]), np.float32)]
        kvn_pre = [np.full((128, NPRE[0]), NEGBIG, np.float32),
                   np.full((128, NPRE[1]), NEGBIG, np.float32)]
        for it, (bb, j) in enumerate(_core_blocks(c)):
            s = j * QB
            lo = max(0, s - LB)
            off = lo - (s - LB)
            x_span[it, off:] = x[bb, lo:s + QB]
            x_spanT[it, :, :, off:] = (x[bb, lo:s + QB].T
                                       .reshape(16, 128, SPAN - off)
                                       .transpose(1, 0, 2))
            kv = np.full(SPAN, NEGBIG, np.float32)
            kv[off:] = 0.0
            kvn_span[it] = kv.reshape(NSC, 128).T
            pos = np.zeros(SPAN, np.float32)
            pos[off:] = np.arange(lo, s + QB, dtype=np.float32)
            posb[it, :NSC] = pos[::128]
            plen = lo
            xp[it][:, :, :plen] = (x[bb, :plen].T
                                   .reshape(16, 128, plen).transpose(1, 0, 2))
            kvp = np.full(P_PAD[it], NEGBIG, np.float32)
            kvp[:plen] = 0.0
            kvn_pre[it] = kvp.reshape(NPRE[it], 128).T
            pp = np.zeros(P_PAD[it], np.float32)
            pp[:plen] = np.arange(plen, dtype=np.float32)
            posb[it, NSC:NSC + NPRE[it]] = pp[::128]
        in_maps.append({
            "x_span": x_span, "x_spanT": x_spanT,
            "x_pre_lo": xp[0], "x_pre_hi": xp[1],
            "kvn_span": np.ascontiguousarray(kvn_span),
            "kvn_pre_lo": kvn_pre[0], "kvn_pre_hi": kvn_pre[1],
            "posb": posb,
            "WqT": WqT, "WkT": WkT, "WvT": WvT, "WoT": WoT, "WfT": WfT,
            "bo": bo, "prot": prot, "diag": diag, "sgn": sgn, "ident": ident,
        })
    return in_maps


def gather_output(results):
    out = np.zeros((B, L, H), np.float32)
    for c in range(8):
        o = np.asarray(results[c]["out"], np.float32)
        for it, (bb, j) in enumerate(_core_blocks(c)):
            out[bb, j * QB:(j + 1) * QB] = o[it]
    return out



class Arena:
    def __init__(self, nc, size_bytes):
        self.nc, self.size, self.items = nc, size_bytes, []

    def at(self, name, shape, dtype, off):
        per = int(np.prod(shape[1:])) * mybir.dt.size(dtype)
        assert off % 32 == 0 and off + per <= self.size, (name, off, per)
        h = self.nc.virtual_sbuf_tensor(name, list(shape), dtype)
        self.items.append((h, off))
        return h

    def commit(self, es):
        big = es.enter_context(
            self.nc.sbuf_tensor("arena", [128, self.size], mybir.dt.uint8))
        base = self.nc.lookup_mloc(big).addr
        for h, off in self.items:
            self.nc._tensor(h.name, list(h.shape), h.dtype, type="SB")
            mloc = self.nc.lookup_mloc(h)
            mloc.allocated = not self.nc.target_bir_lowering
            mloc.addr = base + off
        return big


A_K = 0
A_V = 49_152
A_Q = 99_072
A_COSP, A_SINP = 0, 13_312
A_WBIG = 0
A_W6, A_R6 = 65_536, 73_728
A_OUT1 = 81_920
A_FIN = 90_112
A_XQ = 98_432
A_FIN2 = 114_688
A_XQT = 65_536
O_KG = 131_968
O_VG = 145_280
O_WF = 131_968
O_BO = 140_160
O_OUT1R = 131_968
O_WF2 = 148_352
O_ATTNT = 158_720
O_XT = 158_720
O_XN = 162_816
O_XNT = 166_912
O_WSL = 183_296
O_XTB = 191_488
O_XNB = 195_584
O_BIAS_S = 195_584
O_BIAS_P = 197_120
O_ELOR = 197_536
O_EHIR = 198_048
O_ONESR = 198_560
O_DIAGB = 199_072
O_ONESB = 199_328
O_AB = 191_488
O_ROPE = 199_936
O_KU, O_T1R, O_T2R, O_VFM = 199_936, 200_960, 201_984, 203_008
O_THA, O_THB, O_THG = 199_936, 200_192, 200_448
O_IVF, O_RROW, O_POSR = 200_704, 201_216, 201_728
O_RECIPS = 199_936
O_FACT = 201_984
O_AU = 203_008
O_CSC = 204_032
O_CSS = 205_056
O_T1C, O_T1S, O_T1NS = 206_080, 206_336, 206_592
O_CG = 206_848
O_T2C, O_T2S = 207_360, 207_872
O_STATS = 208_384
O_NRM = 208_512
O_SC1 = 208_768
O_GAM = 209_024
O_PERM = 209_056
O_SQN = 209_088
O_PROTB = 210_112
O_IDENT = 210_368
O_BPI = 210_624
O_ZC = 210_656
O_EPSC = 210_688
O_SGNC = 210_720
O_ELOC = 210_752
O_EHIC = 210_784
O_KVS = 210_816
O_KVP = 210_912
O_VONES = 211_136
O_SWK = 211_168
O_SWV = 211_424
O_SQB = 191_488
O_ONES32 = 211_680
ARENA_SZ = 212_192


SEM_NAMES = ["dX", "dT", "dW", "dO", "tS", "vS", "sS", "gS"]
SERIAL = {"vector": "vS", "scalar": "sS", "gpsimd": "gS"}


class Prog:

    def __init__(self, nc, bufs, sems, eng_name, engine, ktabs=None):
        self.nc, self.b, self.sems = nc, bufs, sems
        self.cnt = {k: 0 for k in SEM_NAMES}
        self.eng_name, self.e = eng_name, engine
        self.ktabs = ktabs
        self.kidx = {"sync": 0, "gpsimd": 0}
        self.krec = {"sync": [], "gpsimd": []}

    def on(self, owner):
        return self.eng_name == owner

    def w(self, owner, sem, val):
        if self.on(owner) and val > 0:
            self.e.wait_ge(self.sems[sem], val)

    def op(self, owner, fn, n=1):
        sem = SERIAL[owner]
        pre = self.cnt[sem]
        tok = self.cnt[sem] = pre + n
        if self.on(owner):
            if pre > 0:
                self.e.wait_ge(self.sems[sem], pre)
            inst = fn(self.e)
            inst.then_inc(self.sems[sem], n)
        return tok

    def mm(self, fn, tok=False):
        if tok:
            t = self.cnt["tS"] = self.cnt["tS"] + 1
            if self.on("tensor"):
                fn(self.e).then_inc(self.sems["tS"], 1)
            return t
        if self.on("tensor"):
            fn(self.e)
        return None

    def _dma_k(self, owner, emit):
        if self.ktabs is None:
            k = 1
            if self.on(owner):
                n0 = self.nc.n_instructions()
                emit()
                k = self.nc.n_instructions() - n0
                self.krec[owner].append(k)
            return k
        k = self.ktabs[owner][self.kidx[owner]]
        self.kidx[owner] += 1
        if self.on(owner):
            emit()
        return k

    def dma(self, owner, sem, out, in0):
        k = self._dma_k(owner, lambda: self.e.dma_start(out=out, in_=in0)
                        .then_inc(self.sems[sem], 16))
        tok = self.cnt[sem] = self.cnt[sem] + 16 * k
        if self.on(owner):
            self.e.wait_ge(self.sems[sem], tok)
        return tok

    def dmaT(self, out, in0):
        k = self._dma_k("sync", lambda: self.e.dma_start_transpose(out, in0)
                        .then_inc(self.sems["dT"], 16))
        tok = self.cnt["dT"] = self.cnt["dT"] + 16 * k
        if self.on("sync"):
            self.e.wait_ge(self.sems["dT"], tok)
        return tok

    def fence(self):
        snap = dict(self.cnt)
        own = {"vector": "vS", "scalar": "sS", "gpsimd": "gS", "tensor": "tS"}.get(self.eng_name)
        for sem, val in snap.items():
            if sem != own and val > 0:
                if self.e is not None:
                    self.e.wait_ge(self.sems[sem], val)
        return snap


def build_program(ktabs=None):
    nc = bass.Bass("TRN2", target_bir_lowering=False)
    dram = {}

    def din(name, shape, dtype=F32):
        dram[name] = nc.dram_tensor(name, list(shape), dtype, kind="ExternalInput").ap()

    din("x_span", [2, SPAN, H]); din("x_spanT", [2, 128, 16, SPAN])
    din("x_pre_lo", [128, 16, P_PAD[0]]); din("x_pre_hi", [128, 16, P_PAD[1]])
    din("kvn_span", [2, 128, NSC]); din("kvn_pre_lo", [128, NPRE[0]]); din("kvn_pre_hi", [128, NPRE[1]])
    din("posb", [2, 64])
    din("WqT", [8, 128, 2, 16, 128]); din("WkT", [8, 128, 2, 16, 128]); din("WvT", [8, 128, 2, 16, 128])
    din("WoT", [16, 128, 2048]); din("WfT", [16, 128, 2048])
    din("bo", [1, 2048]); din("prot", [128, 128]); din("diag", [128, 128]); din("sgn", [128, 1])
    din("ident", [128, 128])
    dram["out"] = nc.dram_tensor("out", [2, QB, H], F32, kind="ExternalOutput").ap()
    dram["out1_scr"] = nc.dram_tensor("out1_scr", [QB, H], F32, kind="Internal").ap()
    dram["wq_hbm"] = nc.dram_tensor("wq_hbm", [128, 16, 2048], BF16, kind="Internal").ap()
    dram["wb_hbm"] = nc.dram_tensor("wb_hbm", [3, 8, 128, 2, 16, 128], BF16,
                                    kind="Internal").ap()

    es = ExitStack()
    ar = Arena(nc, ARENA_SZ)
    bufs = {"dram": dram}

    def ab(name, shape, dtype, off):
        bufs[name] = ar.at(name, shape, dtype, off)

    ab("K", [128, 16, SPAN], BF16, A_K)
    ab("V", [128, V_FLAT], BF16, A_V)
    ab("Q", [128, 16, QB], BF16, A_Q)
    ab("cosp", [128, P_PAD[1]], BF16, A_COSP)
    ab("sinp", [128, P_PAD[1]], BF16, A_SINP)
    ab("wbig", [128, 16, 2048], BF16, A_WBIG)
    ab("w6", [128, 2048], F32, A_W6)
    ab("r6", [128, 2048], F32, A_R6)
    ab("out1", [128, 2048], F32, A_OUT1)
    ab("xq", [128, 8, 2048], BF16, A_XQ)
    ab("fin", [128, 2048], F32, A_FIN)
    ab("fin2", [128, 2, 2048], F32, A_FIN2)
    ab("xqT", [128, 16, QB], BF16, A_XQT)
    ab("Kg", [128, P_PAD[1]], BF16, O_KG)
    ab("Vg", [128, VG_FLAT], BF16, O_VG)
    ab("wf", [128, 2048], F32, O_WF)
    ab("bo_sb", [1, 2048], BF16, O_BO)
    ab("out1r", [128, 2, 2048], F32, O_OUT1R)
    ab("wf2", [128, 2, 2048], F32, O_WF2)
    ab("attnT", [128, 16, QB], BF16, O_ATTNT)
    ab("sqs", [128, 2, 512], BF16, O_XT)
    ab("rows", [33, 512], F32, O_XT + 2_048)
    ab("rows2", [33, 512], F32, O_XT + 4_096)
    ab("kuB", [128, 512], BF16, O_XT + 6_144)
    ab("sqnB", [128, 512], BF16, O_XT + 7_168)
    ab("wfb", [128, 2048], F32, O_XT)
    ab("sqb", [128, 8, 512], BF16, O_SQB)
    ab("swk", [1, 128], BF16, O_SWK)
    ab("swv", [1, 128], BF16, O_SWV)
    ab("onesP", [1, 128], F32, O_ONES32)
    ab("xnT", [128, 16, 512], BF16, O_XNT)
    ab("wsl", [128, 2, 2, 16, 128], BF16, O_WSL)
    ab("bias_s", [128, 32, NSC], F32, O_BIAS_S)
    ab("bias_p", [128, 2, NPRE[1]], F32, O_BIAS_P)
    ab("kvs", [128, 2, NSC], F32, O_KVS)
    ab("kvp", [128, NPRE[1]], F32, O_KVP)
    ab("vones", [128, 1], BF16, O_VONES)
    ab("eloR", [33, 128], F32, O_ELOR)
    ab("onesr", [1, 128], F32, O_ONESR)
    ab("onesb", [1, 128], BF16, O_ONESB)
    ab("diag_b", [128, 128], BF16, O_DIAGB)
    ab("Ab", [128, 4, 512], BF16, O_AB)
    ab("sqn", [128, 512], BF16, O_SQN)
    ab("prot_b", [128, 128], BF16, O_PROTB)
    ab("ident_b", [128, 128], BF16, O_IDENT)
    ab("bpi", [128, 1], F32, O_BPI)
    ab("zc", [128, 1], F32, O_ZC)
    ab("epsc", [128, 1], F32, O_EPSC)
    ab("sgn_c", [128, 1], F32, O_SGNC)
    ab("eloC", [128, 1], BF16, O_ELOC)
    ab("ehiC", [128, 1], BF16, O_EHIC)
    ab("ku", [128, 512], BF16, O_KU)
    ab("t1r", [128, 512], BF16, O_T1R)
    ab("t2r", [128, 512], BF16, O_T2R)
    ab("vfm", [128, 512], BF16, O_VFM)
    ab("tha", [128, 64], F32, O_THA)
    ab("thb", [128, 64], F32, O_THB)
    ab("thg", [128, 64], F32, O_THG)
    ab("ivf", [1, 128], F32, O_IVF)
    ab("rrow", [1, 128], F32, O_RROW)
    ab("posr", [1, 64], F32, O_POSR)
    ab("recips", [33, 512], F32, O_RECIPS)
    ab("fact", [128, 512], BF16, O_FACT)
    ab("attnU", [128, 512], BF16, O_AU)
    ab("cs_c", [128, 512], BF16, O_CSC)
    ab("cs_s", [128, 512], BF16, O_CSS)
    ab("T1c", [128, 64], F32, O_T1C)
    ab("T1s", [128, 64], F32, O_T1S)
    ab("T1ns", [128, 64], F32, O_T1NS)
    ab("cg", [128, 128], F32, O_CG)
    ab("T2c", [128, 128], F32, O_T2C)
    ab("T2s", [128, 128], F32, O_T2S)
    ab("stats", [128, 32], F32, O_STATS)
    ab("nrm", [1, 64], F32, O_NRM)
    ab("sc1", [1, 64], F32, O_SC1)
    ab("gam", [128, 8], F32, O_GAM)
    ab("perm", [128, 8], F32, O_PERM)

    ar.commit(es)
    bufs["pb"] = [es.enter_context(nc.psum_tensor(f"pb{i}", [128, 512], F32))
                  for i in range(8)]
    sems = {k: es.enter_context(nc.semaphore(k)) for k in SEM_NAMES}
    block = es.enter_context(nc.Block())

    counts = {}
    krec = {}

    def section(eng_name):
        def run(e):
            p = Prog(nc, bufs, sems, eng_name, e, ktabs)
            walk(p)
            counts[eng_name] = dict(p.cnt)
            if ktabs is None and eng_name in ("sync", "gpsimd"):
                krec[eng_name] = p.krec[eng_name]
        return run

    block.sync(section("sync"))
    block.tensor(section("tensor"))
    block.scalar(section("scalar"))
    block.vector(section("vector"))
    block.gpsimd(section("gpsimd"))
    if ktabs is None:
        es.close()
        return krec
    ref = counts["sync"]
    for k, v in counts.items():
        assert v == ref, (k, v, ref)
    return nc, es



def walk(p: Prog):
    import os
    KPH = int(os.environ.get("KPH", "9"))
    KIT = int(os.environ.get("KIT", "2"))
    _scopes = {}

    def ph0(name):
        if KPH >= 9:
            _scopes[name] = p.nc.enter_named_scope(name, False)[0]

    def ph1(name):
        if KPH >= 9:
            p.nc.leave_named_scope(name, _scopes.pop(name), False)
    b = p.b
    d = b["dram"]
    pb = b["pb"]

    def vop(fn, n=1):
        return p.op("vector", fn, n)

    def sop(fn, n=1):
        return p.op("scalar", fn, n)

    def gop(fn, n=1):
        return p.op("gpsimd", fn, n)

    def ts_(out, in0, s1, op0, s2=None, op1=None):
        if s2 is None and op1 is None:
            return lambda e: e.tensor_single_scalar(out, in0, s1, op0)
        return lambda e: e.tensor_scalar(out, in0, s1, s2, op0, op1)

    def stt_(out, in0, s, in1, op0, op1):
        return lambda e: e.scalar_tensor_tensor(out, in0, s, in1, op0, op1)

    def tt_(out, in0, in1, op):
        return lambda e: e.tensor_tensor(out, in0, in1, op)

    def act_(out, in0, func, bias, scale=1.0, accum=None):
        return lambda e: e.activation(out, in0, func, bias=bias, scale=scale,
                                      accum_out=accum)

    def red_(out, in0, op, absval=False):
        return lambda e: e.tensor_reduce(out, in0, axis=AX.X, op=op,
                                         apply_absolute_value=absval)

    def mm_(out, lhsT, rhs, start, stop):
        return lambda e: e.matmul(out, lhsT, rhs, start=start, stop=stop,
                                  skip_group_check=True)

    def ladder_v(src, dst, tmp, levels):
        vop(ts_(tmp, src, float(2 ** (levels - 1)) - 0.5, ALU.is_ge))
        vop(ts_(dst, tmp, float(2 ** (levels - 1)), ALU.mult))
        for i in range(levels - 2, -1, -1):
            vop(stt_(tmp, dst, float(2 ** i) - 0.5, src, ALU.add, ALU.is_le))
            t = vop(stt_(dst, tmp, float(2 ** i), dst, ALU.mult, ALU.add))
        return t

    t_sgn = p.dma("sync", "dX", b["sgn_c"][:], d["sgn"][:])
    gop(lambda e: e.memset(b["perm"][:], 0.0))
    gop(lambda e: e.memset(b["vones"][:], 1.0))
    p.dma("gpsimd", "dW", b["ident_b"][:], d["ident"][:])
    p.fence()

    for it in range(KIT):
        npre, npt, ppad = NPRE[it], NPT[it], P_PAD[it]
        x_pre = d["x_pre_lo"] if it == 0 else d["x_pre_hi"]
        kvn_pre = d["kvn_pre_lo"] if it == 0 else d["kvn_pre_hi"]

        ph0(f"i{it}_tab")
        t_prot = p.dma("gpsimd", "dW", b["prot_b"][:], d["prot"][:])
        gop(lambda e: e.memset(b["bpi"][:], 1.5707963267948966))
        gop(lambda e: e.memset(b["zc"][:], 0.0))
        gop(lambda e: e.memset(b["epsc"][:], 1e-5))
        gop(lambda e: e.memset(b["onesP"][0:1, :], 1.0))
        gop(lambda e: e.memset(b["eloC"][0:64, :], 1.0))
        gop(lambda e: e.memset(b["eloC"][64:128, :], 0.0))
        gop(lambda e: e.memset(b["ehiC"][0:64, :], 0.0))
        gop(lambda e: e.memset(b["ehiC"][64:128, :], 1.0))
        gop(lambda e: e.memset(b["nrm"][:], 0.0))
        t_pos = p.dma("sync", "dX", b["posr"][:], d["posb"][it:it + 1, 0:64])

        gop(lambda e: e.iota(b["rrow"][:], pattern=[[0, 4], [1, 32]], base=0, channel_multiplier=0,
                             allow_small_or_imprecise_dtypes=True))
        t = p.cnt["gS"]
        p.w("scalar", "gS", t)
        sop(act_(b["ivf"][:], b["rrow"][:], AF.Exp, b["zc"][0:1, :], -LN10K_32))
        t = p.cnt["sS"]
        p.w("gpsimd", "sS", t)
        gop(lambda e: e.iota(b["rrow"][:], pattern=[[1, 128]], base=0, channel_multiplier=0,
                             allow_small_or_imprecise_dtypes=True))
        p.w("tensor", "gS", p.cnt["gS"])
        p.w("tensor", "sS", p.cnt["sS"])
        t_mm = p.mm(mm_(pb[6][:, 0:128], b["ivf"][:], b["rrow"][:], True, True), tok=True)
        p.w("vector", "tS", t_mm)

        tabst = {"sin_done": 0}

        def sincos_from_theta(th_psum, w, dc, ds, k_sc, lad_lv):
            p.w("vector", "sS", tabst["sin_done"])
            vop(ts_(b["tha"][:, 0:w], th_psum, 1.0 / TWO_PI, ALU.mult))
            ladder_v(b["tha"][:, 0:w], b["thb"][:, 0:w], b["thg"][:, 0:w], lad_lv)
            vop(stt_(b["tha"][:, 0:w], b["thb"][:, 0:w], -CW_C1, th_psum,
                     ALU.mult, ALU.add))
            vop(stt_(b["tha"][:, 0:w], b["thb"][:, 0:w], -CW_C2,
                     b["tha"][:, 0:w], ALU.mult, ALU.add))
            vop(ts_(b["thg"][:, 0:w], b["tha"][:, 0:w], math.pi / 2, ALU.add))
            vop(ts_(b["thb"][:, 0:w], b["thg"][:, 0:w], math.pi, ALU.is_gt))
            t = vop(stt_(b["thg"][:, 0:w], b["thb"][:, 0:w], -TWO_PI,
                         b["thg"][:, 0:w], ALU.mult, ALU.add))
            p.w("scalar", "vS", t)
            sop(act_(dc, b["thg"][:, 0:w], AF.Sin, b["zc"][:, 0:1]))
            tok = sop(act_(ds, b["tha"][:, 0:w], AF.Sin, b["zc"][:, 0:1]))
            tabst["sin_done"] = tok
            return tok

        t_s = sincos_from_theta(pb[6][:, 0:64], 64, b["T2c"][:, 0:64],
                                b["T2s"][:, 0:64], None, 5)
        t_s = sincos_from_theta(pb[6][:, 64:128], 64, b["T2c"][:, 64:128],
                                b["T2s"][:, 64:128], None, 5)
        p.w("tensor", "dX", t_pos)
        p.w("tensor", "sS", t_s)
        p.w("tensor", "vS", p.cnt["vS"])
        t_mm = p.mm(mm_(pb[6][:, 0:64], b["ivf"][:], b["posr"][:], True, True), tok=True)
        p.w("vector", "tS", t_mm)
        t_s = sincos_from_theta(pb[6][:, 0:64], 64, b["T1c"][:, 0:64],
                                b["T1s"][:, 0:64], None, 11)
        p.w("vector", "sS", t_s)
        vop(ts_(b["T1ns"][:, 0:64], b["T1s"][:, 0:64], -1.0, ALU.mult))

        def cosgen(col, dstc, dsts):
            vop(ts_(b["cg"][:], b["T2c"][:], b["T1c"][:, col:col + 1], ALU.mult))
            vop(stt_(dstc, b["T2s"][:], b["T1ns"][:, col:col + 1], b["cg"][:],
                     ALU.mult, ALU.add))
            vop(ts_(b["cg"][:], b["T2s"][:], b["T1c"][:, col:col + 1], ALU.mult))
            vop(stt_(b["cg"][:], b["T2c"][:], b["T1s"][:, col:col + 1], b["cg"][:],
                     ALU.mult, ALU.add))
            return vop(ts_(dsts, b["cg"][:], b["sgn_c"][:, 0:1], ALU.mult))

        p.w("vector", "dX", t_sgn)
        for bi in range(npre):
            cosgen(NSC + bi, b["cosp"][:, bi * 128:(bi + 1) * 128],
                   b["sinp"][:, bi * 128:(bi + 1) * 128])
        p.fence()

        state = {"xn_tr": 0, "xn_tr_last": 0, "proj_done": 0,
                 "sq_done": 0, "red_done": 0, "sq_junk": 0, "ts3": 0}

        def ln_subtile(src_row_ap, col0):
            xt, xn = b["xt"], b["xn"]
            p.w("gpsimd", "vS", max(state["sq_done"], state["red_done"]))
            p.w("gpsimd", "sS", state["sq_junk"])
            t_x = p.dma("gpsimd", "dW", xt[:], src_row_ap)
            p.w("vector", "dW", t_x)
            state["red_done"] = vop(red_(b["stats"][:, 0:1], xt[:], ALU.add))
            vop(ts_(b["stats"][:, 2:3], b["stats"][:, 0:1], 1.0 / 2048.0, ALU.mult))
            p.w("scalar", "dW", t_x)
            p.w("scalar", "dT", state["xn_tr"])
            p.w("scalar", "vS", state["ts3"])
            t_sq = sop(act_(xn[:], xt[:], AF.Square, b["zc"][:, 0:1], 1.0,
                            b["stats"][:, 1:2]))
            state["sq_junk"] = t_sq
            p.w("vector", "sS", t_sq)
            state["ts3"] = vop(ts_(b["stats"][:, 3:4], b["stats"][:, 1:2],
                                   1.0 / 2048.0, ALU.mult))
            vop(tt_(b["stats"][:, 4:5], b["stats"][:, 2:3], b["stats"][:, 2:3], ALU.mult))
            t_v = vop(tt_(b["stats"][:, 5:6], b["stats"][:, 3:4], b["stats"][:, 4:5],
                          ALU.subtract))
            p.w("scalar", "vS", t_v)
            t_s = sop(act_(b["stats"][:, 6:7], b["stats"][:, 5:6], AF.Sqrt,
                           b["epsc"][:, 0:1]))
            p.w("vector", "sS", t_s)
            vop(lambda e: e.reciprocal(b["stats"][:, 7:8], b["stats"][:, 6:7]))
            vop(stt_(b["stats"][:, 8:9], b["stats"][:, 2:3], -1.0,
                     b["stats"][:, 7:8], ALU.mult, ALU.mult))
            state["sq_done"] = vop(ts_(xn[:], xt[:], b["stats"][:, 7:8],
                                       ALU.mult, b["stats"][:, 8:9], ALU.add))
            p.w("sync", "vS", state["sq_done"])
            p.w("sync", "tS", state["proj_done"])
            state["xn_tr"] = p.dmaT(b["xnT"][:, :, col0:col0 + 128], xn[:])
            state["xn_tr_last"] = state["xn_tr"]
            return state["xn_tr_last"]

        ph1(f"i{it}_tab")
        if KPH < 2:
            p.fence()
            continue
        ph0(f"i{it}_pre")
        w15_sem = "dW" if it == 0 else "dX"
        if it == 0:
            t_w15 = p.dma("gpsimd", "dW", b["wsl"][:, 0, 0], d["WkT"][7, :, 1])
            t_w15 = p.dma("gpsimd", "dW", b["wsl"][:, 0, 1], d["WvT"][7, :, 1])
        else:
            t_w15 = p.dma("sync", "dX", b["wsl"][:, 0, 0], d["wb_hbm"][0, 7][:, 1])
            t_w15 = p.dma("sync", "dX", b["wsl"][:, 0, 1], d["wb_hbm"][1, 7][:, 1])
        pstate = {"pb_free": [0, 0], "rot_free": [0, 0], "ku_read": [0, 0],
                  "sqn_free": [0, 0], "nrm_free": 0, "vfm_tr": 0}

        def norms(dst_slice, qcol0, t_ready, w=512, sq=0):
            sqn = b["sqnB"] if sq else b["sqn"]
            p.w("gpsimd", "tS", pstate["sqn_free"][sq])
            p.w("gpsimd", "vS", t_ready)
            t_sq = gop(tt_(sqn[:, 0:w], dst_slice, dst_slice, ALU.mult))
            p.w("tensor", "gS", t_sq)
            p.w("tensor", "vS", pstate["nrm_free"])
            p.mm(mm_(pb[7][0:1, 0:w], b["eloC"][:], sqn[:, 0:w], True, True))
            t_nm = p.mm(mm_(pb[6][0:1, 0:w], b["ehiC"][:], sqn[:, 0:w],
                            True, True), tok=True)
            pstate["sqn_free"][sq] = t_nm
            p.w("vector", "tS", t_nm)
            vop(red_(b["sc1"][0:1, 0:1], pb[7][0:1, 0:w], ALU.max))
            vop(tt_(b["nrm"][0:1, qcol0:qcol0 + 1], b["nrm"][0:1, qcol0:qcol0 + 1],
                    b["sc1"][0:1, 0:1], ALU.max))
            vop(red_(b["sc1"][0:1, 0:1], pb[6][0:1, 0:w], ALU.max))
            pstate["nrm_free"] = vop(tt_(
                b["nrm"][0:1, qcol0 + 1:qcol0 + 2], b["nrm"][0:1, qcol0 + 1:qcol0 + 2],
                b["sc1"][0:1, 0:1], ALU.max))

        def rope_evac(pp, cos_sl, sin_sl, dst_slice, w=256, rp=0, rstd_ps=None,
                      kb=0):
            ku = b["kuB"] if kb else b["ku"]
            p.w("scalar", "vS", pstate["ku_read"][kb])
            t_ku = sop(lambda e: e.copy(ku[:, 0:w], pb[pp][:, 0:w]))
            pstate["pb_free"][pp] = t_ku
            p.w("tensor", "sS", t_ku)
            p.w("tensor", "vS", pstate["rot_free"][rp])
            t_rot = p.mm(mm_(pb[2 + rp][:, 0:w], b["prot_b"][:], ku[:, 0:w],
                             True, True), tok=True)
            p.w("vector", "sS", t_ku)
            pstate["ku_read"][kb] = vop(tt_(b["t1r"][:, 0:w], ku[:, 0:w], cos_sl,
                                            ALU.mult))
            p.w("vector", "tS", t_rot)
            pstate["rot_free"][rp] = vop(tt_(b["t2r"][:, 0:w], pb[2 + rp][:, 0:w],
                                             sin_sl, ALU.mult))
            if rstd_ps is None:
                return vop(tt_(dst_slice, b["t1r"][:, 0:w], b["t2r"][:, 0:w],
                               ALU.add))
            vop(tt_(b["t1r"][:, 0:w], b["t1r"][:, 0:w], b["t2r"][:, 0:w], ALU.add))
            return vop(tt_(dst_slice, b["t1r"][:, 0:w], rstd_ps[:, 0:w], ALU.mult))

        p.w("tensor", w15_sem, t_w15)
        t = 0
        for hc in range(16):
            t = p.mm(mm_(pb[4][0:1, 0:128], b["vones"][:], b["wsl"][:, 0, 0, hc, :],
                         hc == 0, hc == 15), tok=(hc == 15))
        p.w("scalar", "tS", t)
        t_evk = sop(lambda e: e.copy(b["swk"][:], pb[4][0:1, 0:128]))
        p.w("tensor", "sS", t_evk)
        for hc in range(16):
            t = p.mm(mm_(pb[4][0:1, 0:128], b["vones"][:], b["wsl"][:, 0, 1, hc, :],
                         hc == 0, hc == 15), tok=(hc == 15))
        p.w("scalar", "tS", t)
        t_evv = sop(lambda e: e.copy(b["swv"][:], pb[4][0:1, 0:128]))
        pstate.update(pb4_free=t_evv, stats_read=0, rstd_read=0,
                      xnT_users=0, sq_users=[0, 0], sq_tok=[0, 0])

        for pt in range(npt):
            T0 = pt * 512
            p.w("gpsimd", "tS", pstate["xnT_users"])
            p.w("gpsimd", "sS", max(pstate["sq_tok"]))
            p.w("gpsimd", "dT", pstate["vfm_tr"])
            t_xp = p.dma("gpsimd", "dW", b["xnT"][:], x_pre[:, :, T0:T0 + 512])
            p.w("tensor", "dW", t_xp)
            p.w("tensor", "vS", pstate["stats_read"])
            p.w("tensor", "sS", pstate["pb4_free"])
            for hc in range(16):
                p.mm(mm_(pb[4][0:1, 0:512], b["vones"][:], b["xnT"][:, hc, :],
                         hc == 0, hc == 15))
            t_sqmm = 0
            for half in range(2):
                p.w("scalar", "dW", t_xp)
                p.w("scalar", "tS", max(pstate["sq_users"]))
                p.w("scalar", "dT", pstate["vfm_tr"])
                t_sq = sop(act_(b["sqb"][:], b["xnT"][:, 8 * half:8 * half + 8],
                                AF.Square, b["zc"][:, 0:1]))
                pstate["sq_tok"][half] = t_sq
                p.w("tensor", "sS", t_sq)
                for hc in range(8):
                    t_sqmm = p.mm(mm_(pb[5][0:1, 0:512], b["vones"][:],
                                      b["sqb"][:, hc, :],
                                      half == 0 and hc == 0,
                                      half == 1 and hc == 7), tok=(hc == 7))
                pstate["sq_users"][half] = t_sqmm
            p.w("vector", "tS", t_sqmm)
            p.w("vector", "dT", pstate["vfm_tr"])
            vop(ts_(b["recips"][0:1, :], pb[4][0:1, :], 1.0 / 2048.0, ALU.mult))
            v_nm = vop(ts_(b["attnU"][0:1, :], b["recips"][0:1, :], -1.0, ALU.mult))
            vop(tt_(b["recips"][0:1, :], b["recips"][0:1, :], b["recips"][0:1, :],
                    ALU.mult))
            t_var = vop(stt_(b["recips"][0:1, :], pb[5][0:1, :], 1.0 / 2048.0,
                             b["recips"][0:1, :], ALU.mult, ALU.subtract))
            pstate["stats_read"] = t_var
            p.w("scalar", "vS", t_var)
            t_sd = sop(act_(b["recips"][32:33, :], b["recips"][0:1, :], AF.Sqrt,
                            b["epsc"][32:33, 0:1]))
            p.w("vector", "sS", t_sd)
            t_rst = vop(lambda e: e.reciprocal(b["recips"][0:1, :],
                                               b["recips"][32:33, :]))
            p.w("tensor", "vS", max(t_rst, pstate["rstd_read"]))
            p.mm(mm_(pb[3][:, 0:512], b["onesP"][:], b["recips"][0:1, :],
                     True, True))
            p.w("tensor", "sS", pstate["pb_free"][0])
            for hc in range(16):
                p.mm(mm_(pb[0][:, 0:512], b["wsl"][:, 0, 0, hc, :],
                         b["xnT"][:, hc, :], hc == 0, False))
            t_mm = p.mm(mm_(pb[0][:, 0:512], b["swk"][:], b["attnU"][0:1, :],
                            False, True), tok=True)
            p.w("scalar", "tS", t_mm)
            t_k = rope_evac(0, b["cosp"][:, T0:T0 + 512], b["sinp"][:, T0:T0 + 512],
                            b["Kg"][:, T0:T0 + 512], w=512, rstd_ps=pb[3])
            norms(b["Kg"][:, T0:T0 + 512], 32 + 30, t_k)
            p.w("tensor", "sS", pstate["pb_free"][1])
            for hc in range(16):
                p.mm(mm_(pb[1][:, 0:512], b["wsl"][:, 0, 1, hc, :],
                         b["xnT"][:, hc, :], hc == 0, False))
            t_mm2 = p.mm(mm_(pb[1][:, 0:512], b["swv"][:], b["attnU"][0:1, :],
                             False, True), tok=True)
            pstate["xnT_users"] = t_mm2
            p.w("scalar", "tS", t_mm2)
            p.w("scalar", "dT", pstate["vfm_tr"])
            t_vf = sop(lambda e: e.copy(b["vfm"][:, 0:512], pb[1][:, 0:512]))
            pstate["pb_free"][1] = t_vf
            p.w("vector", "sS", t_vf)
            t_vv = vop(tt_(b["vfm"][:, 0:512], b["vfm"][:, 0:512], pb[3][:, 0:512],
                           ALU.mult))
            pstate["rstd_read"] = t_vv
            p.w("sync", "vS", t_vv)
            p.w("sync", "dT", pstate["vfm_tr"])
            ci0 = T0 // 128
            pstate["vfm_tr"] = p.dmaT(
                RawAP(b["Vg"], ci0 * 128, [[VG_FLAT, 128], [128, 4], [1, 128]]),
                b["vfm"][:, 0:512])
        p.fence()
        ph1(f"i{it}_pre")

        if KPH < 3:
            p.fence()
            continue
        ph0(f"i{it}_span")
        wstate = {"tok": [0, 0], "used": [0, 0]}
        pstate["vfm_used"] = 0
        pstate["vcp"] = 0
        for ci in range(NSC):
            gop(lambda e, ci=ci: e.memset(
                RawAP(b["V"], ci * 2080 + 64, [[V_FLAT, 128], [65, 32], [1, 1]]), 1.0))

        wstate["stored"] = [0, 0]

        def wslab(wi, Wd, sl, buf, cached):
            p.w("gpsimd" if not cached else "sync", "tS", wstate["used"][buf])
            if cached:
                wstate["tok"][buf] = p.dma("sync", "dX", b["wsl"][:, buf],
                                           d["wb_hbm"][wi, sl])
                return wstate["tok"][buf], "dX"
            p.w("gpsimd", "dX", wstate["stored"][buf])
            wstate["tok"][buf] = p.dma("gpsimd", "dW", b["wsl"][:, buf], Wd[sl])
            return wstate["tok"][buf], "dW"

        sstate = {"norm_done": 0, "xnT_mm": 0, "sq_mm": 0, "sq_tok": 0,
                  "rows_read": 0}
        for tt in range(3):
            for bi in range(4):
                cosgen(tt * 4 + bi, b["cs_c"][:, bi * 128:(bi + 1) * 128],
                       b["cs_s"][:, bi * 128:(bi + 1) * 128])
            p.w("gpsimd", "tS", sstate["xnT_mm"])
            p.w("gpsimd", "vS", p.cnt["vS"])
            p.w("gpsimd", "sS", sstate["sq_tok"])
            p.w("gpsimd", "dT", pstate["vfm_tr"])
            t_xs = p.dma("gpsimd", "dW", b["xnT"][:],
                         d["x_spanT"][it][:, :, tt * 512:(tt + 1) * 512])
            p.w("tensor", "dW", t_xs)
            p.w("tensor", "vS", sstate["rows_read"])
            for hc in range(16):
                p.mm(mm_(pb[3][0:1, 0:512], b["vones"][:], b["xnT"][:, hc, :],
                         hc == 0, hc == 15))
            t_sqmm = 0
            for qt in range(8):
                p.w("scalar", "dW", t_xs)
                p.w("scalar", "tS", sstate["sq_mm"])
                p.w("scalar", "dT", pstate["vfm_tr"])
                t_sq = sop(act_(b["sqs"][:], b["xnT"][:, 2 * qt:2 * qt + 2],
                                AF.Square, b["zc"][:, 0:1]))
                sstate["sq_tok"] = t_sq
                p.w("tensor", "sS", t_sq)
                for hc in range(2):
                    t_sqmm = p.mm(mm_(pb[5][0:1, 0:512], b["vones"][:],
                                      b["sqs"][:, hc, :],
                                      qt == 0 and hc == 0,
                                      qt == 7 and hc == 1), tok=(hc == 1))
                sstate["sq_mm"] = t_sqmm
            p.w("vector", "tS", t_sqmm)
            p.w("vector", "dT", pstate["vfm_tr"])
            vop(ts_(b["rows"][0:1, :], pb[3][0:1, :], 1.0 / 2048.0, ALU.mult))
            vop(tt_(b["rows2"][0:1, :], b["rows"][0:1, :], b["rows"][0:1, :],
                    ALU.mult))
            t_var = vop(stt_(b["rows2"][0:1, :], pb[5][0:1, :], 1.0 / 2048.0,
                             b["rows2"][0:1, :], ALU.mult, ALU.subtract))
            p.w("scalar", "vS", t_var)
            t_sd = sop(act_(b["rows2"][32:33, :], b["rows2"][0:1, :], AF.Sqrt,
                            b["epsc"][32:33, 0:1]))
            p.w("vector", "sS", t_sd)
            t_rst = vop(lambda e: e.reciprocal(b["rows2"][0:1, :],
                                               b["rows2"][32:33, :]))
            sstate["rows_read"] = t_rst
            p.w("tensor", "vS", t_rst)
            p.mm(mm_(pb[3][:, 0:512], b["onesP"][:], b["rows"][0:1, :],
                     True, True))
            t_bc = p.mm(mm_(pb[5][:, 0:512], b["onesP"][:], b["rows2"][0:1, :],
                            True, True), tok=True)
            p.w("vector", "tS", t_bc)
            t_nrm = 0
            for hc in range(16):
                vop(tt_(b["xnT"][:, hc, :], b["xnT"][:, hc, :], pb[3][:, 0:512],
                        ALU.subtract))
                t_nrm = vop(tt_(b["xnT"][:, hc, :], b["xnT"][:, hc, :],
                                pb[5][:, 0:512], ALU.mult))
            sstate["norm_done"] = t_nrm
            last_mm = 0
            t0 = tt * 512
            wlist = [("K", d["WkT"]), ("V", d["WvT"])] + \
                    ([("Q", d["WqT"])] if tt >= 1 else [])
            for wname, Wd in wlist:
                wi = {"K": 0, "V": 1, "Q": 2}[wname]
                first_use = it == 0 and tt == (0 if wname != "Q" else 1)
                for sl in range(8):
                    buf = sl % 2
                    t_w, w_sem = wslab(wi, Wd, sl, buf, not first_use)
                    for q in range(2):
                        pc = 2 * sl + q
                        pp = pc % 2
                        p.w("tensor", "vS", sstate["norm_done"])
                        p.w("tensor", w_sem, t_w)
                        p.w("tensor", "sS", pstate["pb_free"][pp])
                        for hc in range(15):
                            p.mm(mm_(pb[pp][:, 0:512], b["wsl"][:, buf, q, hc, :],
                                     b["xnT"][:, hc, :], hc == 0, False))
                        t_mm = p.mm(mm_(
                            pb[pp][:, 0:512], b["wsl"][:, buf, q, 15, :],
                            b["xnT"][:, 15, :], False, True), tok=True)
                        last_mm = t_mm
                        wstate["used"][buf] = t_mm
                        if wname == "V":
                            p.w("scalar", "tS", max(t_mm, pstate["vfm_used"]))
                            t_vf = sop(lambda e: e.copy(b["vfm"][:, 0:512],
                                                        pb[pp][:, 0:512]))
                            pstate["pb_free"][pp] = t_vf
                            p.w("tensor", "sS", max(t_vf, pstate["vcp"]))
                            for k in range(4):
                                t_tr = p.mm(mm_(pb[4][:, k * 128:(k + 1) * 128],
                                                b["vfm"][:, k * 128:(k + 1) * 128],
                                                b["ident_b"][:], True, True),
                                            tok=(k == 3))
                            pstate["vfm_used"] = t_tr
                            p.w("scalar", "tS", t_tr)
                            vdst = RawAP(b["V"], tt * 4 * 2080 + pc * 130,
                                         [[V_FLAT, 128], [2080, 4], [65, 2], [1, 64]])
                            pstate["vcp"] = sop(
                                lambda e, vdst=vdst: e.copy(vdst, pb[4][:, 0:512]))
                        else:
                            p.w("scalar", "tS", t_mm)
                            if wname == "K":
                                dst = b["K"][:, pc, t0:t0 + 512]
                                t_k = rope_evac(pp, b["cs_c"][:], b["cs_s"][:],
                                                dst, w=512, rp=pc % 2, kb=pc % 2)
                                norms(dst, 32 + 2 * pc, t_k, sq=pc % 2)
                            else:
                                dst = b["Q"][:, pc, t0 - 512:t0]
                                t_k = rope_evac(pp, b["cs_c"][:], b["cs_s"][:],
                                                dst, w=512, rp=pc % 2, kb=pc % 2)
                                norms(dst, 2 * pc, t_k, sq=pc % 2)
                    if first_use:
                        p.w("sync", "tS", wstate["used"][buf])
                        wstate["stored"][buf] = p.dma(
                            "sync", "dX", d["wb_hbm"][wi, sl], b["wsl"][:, buf])
            sstate["xnT_mm"] = last_mm
        p.fence()

        ph1(f"i{it}_span")
        if KPH < 4:
            p.fence()
            continue
        ph0(f"i{it}_bias")
        t_kvs = p.dma("sync", "dX", b["kvs"][:], d["kvn_span"][:].rearrange("a p c -> p a c"))
        t_kvp = p.dma("sync", "dX", b["kvp"][:, 0:npre], kvn_pre[:])
        t_db = p.dma("gpsimd", "dW", b["diag_b"][:], d["diag"][:])
        gop(lambda e: e.memset(b["eloR"][0:32, :], 0.0))
        gop(lambda e: e.memset(b["eloR"][0:1, 0:64], 1.0))
        gop(lambda e: e.memset(b["eloR"][32:33, 0:64], 0.0))
        gop(lambda e: e.memset(b["eloR"][32:33, 64:128], 1.0))
        gop(lambda e: e.memset(b["recips"][0:32, :], 0.0))
        gop(lambda e: e.memset(b["onesr"][0:1, :], 1.0))
        t_g = gop(lambda e: e.memset(b["onesb"][0:1, :], 1.0))
        p.w("vector", "dX", t_kvp)
        p.w("vector", "gS", t_g)
        bias_evac = 0
        for h in range(32):
            vop(tt_(b["sc1"][0:1, 0:1], b["nrm"][0:1, h:h + 1],
                    b["nrm"][0:1, 32 + h:33 + h], ALU.mult))
            t_v = p.cnt["vS"]
            p.w("scalar", "vS", t_v)
            t_s = sop(act_(b["sc1"][0:1, 1:2], b["sc1"][0:1, 0:1], AF.Sqrt,
                           b["zc"][0:1, 0:1]))
            p.w("vector", "sS", t_s)
            t_v = vop(ts_(b["sc1"][0:1, 2:3], b["sc1"][0:1, 1:2], -0.125, ALU.mult))
            p.w("tensor", "vS", t_v)
            p.w("tensor", "sS", bias_evac)
            t_mm = p.mm(mm_(pb[7][:, 0:1], b["onesr"][:], b["sc1"][0:1, 2:3],
                            True, True), tok=True)
            p.w("scalar", "tS", t_mm)
            bias_evac = sop(lambda e: e.copy(b["stats"][:, 9:10], pb[7][:, 0:1]))
            p.w("vector", "sS", bias_evac)
            vop(ts_(b["bias_s"][:, h, :], b["kvs"][:, it, :], b["stats"][:, 9:10],
                    ALU.add))
            if h >= 30:
                vop(ts_(b["bias_p"][:, h - 30, 0:npre], b["kvp"][:, 0:npre],
                        b["stats"][:, 9:10], ALU.add))
        if it == 0:
            wf_free = 0
            for ic in range(16):
                p.w("sync", "vS", wf_free)
                t_wf = p.dma("sync", "dX", b["wfb"][:], d["WfT"][ic])
                p.w("vector", "dX", t_wf)
                wf_free = vop(red_(b["stats"][:, 16 + ic:17 + ic], b["wfb"][:],
                                   ALU.add, absval=True))
            vop(red_(b["stats"][:, 10:11], b["stats"][:, 16:32], ALU.add))
            t_v = vop(lambda e: e.memset(b["stats"][:, 14:15], 1.0))
            p.w("tensor", "vS", t_v)
            p.w("tensor", "sS", bias_evac)
            t_mm = p.mm(mm_(pb[7][0:1, 0:1], b["stats"][:, 10:11],
                            b["stats"][:, 14:15], True, True), tok=True)
            p.w("scalar", "tS", t_mm)
            t_s = sop(lambda e: e.copy(b["sc1"][0:1, 8:9], pb[7][0:1, 0:1]))
            p.w("tensor", "sS", t_s)
            t_mm = p.mm(mm_(pb[7][:, 4:5], b["onesr"][:], b["sc1"][0:1, 8:9],
                            True, True), tok=True)
            p.w("vector", "tS", t_mm)
            vop(lambda e: e.tensor_copy(b["stats"][:, 11:12], pb[7][:, 4:5]))
            vop(ts_(b["perm"][:, 1:2], b["stats"][:, 11:12],
                    1.0 / (2048.0 * 2048.0), ALU.mult))
            vop(ts_(b["perm"][:, 0:1], b["perm"][:, 1:2], 1e-8, ALU.max))
            vop(lambda e: e.reciprocal(b["perm"][:, 2:3], b["perm"][:, 0:1]))
        p.fence()

        ph1(f"i{it}_bias")
        if KPH < 5:
            p.fence()
            continue
        ph0(f"i{it}_attn")
        exp_gate = [0, 0]
        av_gate = [0, 0, 0, 0]
        diag_gate = [0, 0, 0, 0]
        pb45_free = 0
        fact_free = 0
        sums_free = 0
        sums_evac = 0
        for qg in range(2):
            for pc in range(16):
                glob = pc == 15
                chunks = ([("pre", cp) for cp in range(npre)] if glob else []) + \
                         [("span", ci) for ci in range(min(NSC, 8 + 4 * qg))]
                meta = []
                for idx, (kind, ci) in enumerate(chunks):
                    off = 128 * max(0, ci - 4 - 4 * qg) if kind == "span" else 0
                    meta.append((kind, ci, idx % 2, off))

                def st_(kind, ci, par, off, pc=pc, qg=qg):
                    p.w("tensor", "sS", exp_gate[par])
                    t = 0
                    for h01 in range(2):
                        if kind == "span":
                            kl = b["K"][64 * h01:64 * h01 + 64, pc,
                                        ci * 128:ci * 128 + 128]
                        else:
                            kl = b["Kg"][64 * h01:64 * h01 + 64,
                                         ci * 128:ci * 128 + 128]
                        qr = b["Q"][64 * h01:64 * h01 + 64, pc,
                                    qg * 512 + off:qg * 512 + 512]
                        t = p.mm(mm_(pb[2 * h01 + par][:, off:512], kl, qr,
                                     True, True), tok=(h01 == 1))
                    return t

                st_tok = [0] * len(chunks)
                st_tok[0] = st_(*meta[0])
                for idx, (kind, ci, par, off) in enumerate(meta):
                    first, last = idx == 0, idx == len(chunks) - 1
                    p.w("scalar", "tS", max(st_tok[idx], av_gate[2 * 0 + par],
                                            av_gate[2 * 1 + par]))
                    p.w("scalar", "vS", max(diag_gate[2 * 0 + par],
                                            diag_gate[2 * 1 + par]))
                    t_exp = 0
                    for h01 in range(2):
                        h = 2 * pc + h01
                        bias = (b["bias_s"][:, h, ci:ci + 1] if kind == "span"
                                else b["bias_p"][:, h01, ci:ci + 1])
                        t_exp = sop(act_(b["Ab"][:, 2 * par + h01, off:512],
                                         pb[2 * h01 + par][:, off:512],
                                         AF.Exp, bias, 0.125))
                        exp_gate[par] = t_exp
                    if idx + 1 < len(chunks):
                        st_tok[idx + 1] = st_(*meta[idx + 1])
                    t_diag = 0
                    if kind == "span":
                        dq = ci - 4 - 4 * qg
                        if 0 <= dq < 4:
                            p.w("vector", "sS", t_exp)
                            for h01 in range(2):
                                t_diag = vop(tt_(
                                    b["Ab"][:, 2 * par + h01, dq * 128:dq * 128 + 128],
                                    b["Ab"][:, 2 * par + h01, dq * 128:dq * 128 + 128],
                                    b["diag_b"][:], ALU.mult))
                                diag_gate[2 * h01 + par] = t_diag
                    p.w("tensor", "sS", t_exp)
                    if t_diag:
                        p.w("tensor", "vS", t_diag)
                    if first:
                        p.w("tensor", "vS", pb45_free)
                        p.w("tensor", "sS", sums_evac)
                    for h01 in range(2):
                        if kind == "span":
                            o = ci * 2080 + (2 * pc + h01) * 65
                            vl = RawAP(b["V"], o, [[V_FLAT, 128], [1, 65]])
                            t_av = p.mm(mm_(pb[4 + h01][0:65, off:512], vl,
                                            b["Ab"][:, 2 * par + h01, off:512],
                                            first, last), tok=True)
                        else:
                            o = ci * 128 + 64 * h01
                            vl = RawAP(b["Vg"], o, [[VG_FLAT, 128], [1, 64]])
                            p.mm(mm_(pb[4 + h01][0:64, off:512], vl,
                                     b["Ab"][:, 2 * par + h01, off:512],
                                     first, last))
                            t_av = p.mm(mm_(pb[4 + h01][64:65, off:512], b["vones"][:],
                                            b["Ab"][:, 2 * par + h01, off:512],
                                            first, last), tok=True)
                        av_gate[2 * h01 + par] = t_av
                t_last = av_gate[2 * 1 + (len(chunks) - 1) % 2]
                p.w("scalar", "tS", max(t_last, sums_free))
                sop(lambda e: e.copy(b["recips"][0:1, :], pb[4][64:65, :]))
                sums_evac = sop(lambda e: e.copy(b["recips"][32:33, :],
                                                 pb[5][64:65, :]))
                p.w("tensor", "sS", sums_evac)
                p.w("tensor", "vS", fact_free)
                sums_free = p.mm(mm_(pb[6][:, :], b["eloR"][:], b["recips"][:, :],
                                     True, True), tok=True)
                p.w("vector", "tS", sums_free)

                def _recip(e):
                    with p.nc.allow_low_precision(reason="fact is bf16 by design"):
                        return e.reciprocal(b["fact"][:], pb[6][:, :])
                fact_free = vop(_recip)
                vop(lambda e: e.tensor_copy(b["attnU"][0:64, :], pb[4][0:64, :]))
                pb45_free = vop(lambda e: e.tensor_copy(b["attnU"][64:128, :],
                                                        pb[5][0:64, :]))
                vop(tt_(b["attnT"][:, pc, qg * 512:qg * 512 + 512],
                        b["attnU"][:], b["fact"][:], ALU.mult))
        p.fence()

        ph1(f"i{it}_attn")
        if KPH < 6:
            p.fence()
            continue
        ph0(f"i{it}_wo")
        t_wo_w = p.dma("gpsimd", "dW", b["wbig"][:],
                       d["WoT"][:].rearrange("a p b -> p a b"))
        t_bo = p.dma("gpsimd", "dW", b["bo_sb"][:], d["bo"][:])
        pb_free_wo = 0
        out1_free = (0, 0, 0)
        sq_prev = 0
        for t8 in range(8):
            p.w("sync", "vS", out1_free[0])
            p.w("sync", "sS", out1_free[1])
            p.w("sync", "dX", out1_free[2])
            t_x = p.dma("sync", "dX", b["out1"][:],
                        d["x_span"][it, 512 + t8 * 128:512 + (t8 + 1) * 128, :])
            p.w("tensor", "dW", max(t_wo_w, t_bo))
            p.w("tensor", "vS", pb_free_wo)
            for n in range(4):
                p.mm(mm_(pb[n][:, :], b["onesb"][0:1, :],
                         b["bo_sb"][0:1, n * 512:(n + 1) * 512], True, False))
            t_wo = 0
            for pc in range(16):
                for n in range(4):
                    t_wo = p.mm(mm_(pb[n][:, :], b["attnT"][:, pc, t8 * 128:(t8 + 1) * 128],
                                    b["wbig"][:, pc, n * 512:(n + 1) * 512],
                                    False, pc == 15),
                                tok=(pc == 15 and n == 3))
            p.w("vector", "tS", t_wo)
            p.w("vector", "dX", t_x)
            for n in range(4):
                t_ev = vop(tt_(b["out1"][:, n * 512:(n + 1) * 512],
                               b["out1"][:, n * 512:(n + 1) * 512], pb[n][:, :],
                               ALU.add))
            pb_free_wo = t_ev
            vop(red_(b["stats"][:, 0:1], b["out1"][:], ALU.add))
            vop(red_(b["stats"][:, 12:13], b["out1"][:], ALU.max))
            t_v = vop(red_(b["stats"][:, 13:14], b["out1"][:], ALU.min))
            p.w("scalar", "vS", t_v)
            p.w("scalar", "vS", sq_prev)
            t_sq = sop(act_(b["fin"][:], b["out1"][:], AF.Square, b["zc"][:, 0:1],
                            1.0, b["stats"][:, 1:2]))
            out1_sq = t_sq
            p.w("vector", "sS", t_sq)
            vop(ts_(b["stats"][:, 2:3], b["stats"][:, 0:1], 1.0 / 2048.0, ALU.mult))
            vop(ts_(b["stats"][:, 3:4], b["stats"][:, 1:2], 1.0 / 2048.0, ALU.mult))
            vop(tt_(b["stats"][:, 4:5], b["stats"][:, 2:3], b["stats"][:, 2:3], ALU.mult))
            t_v = vop(tt_(b["stats"][:, 5:6], b["stats"][:, 3:4], b["stats"][:, 4:5],
                          ALU.subtract))
            p.w("scalar", "vS", t_v)
            t_s = sop(act_(b["stats"][:, 6:7], b["stats"][:, 5:6], AF.Sqrt,
                           b["epsc"][:, 0:1]))
            p.w("vector", "sS", t_s)
            vop(lambda e: e.reciprocal(b["stats"][:, 7:8], b["stats"][:, 6:7]))
            vop(tt_(b["stats"][:, 14:15], b["stats"][:, 12:13], b["stats"][:, 2:3],
                    ALU.subtract))
            vop(tt_(b["stats"][:, 15:16], b["stats"][:, 2:3], b["stats"][:, 13:14],
                    ALU.subtract))
            vop(tt_(b["stats"][:, 14:15], b["stats"][:, 14:15], b["stats"][:, 15:16],
                    ALU.max))
            vop(tt_(b["stats"][:, 15:16], b["stats"][:, 14:15], b["stats"][:, 7:8],
                    ALU.mult))
            vop(ts_(b["stats"][:, 9:10], b["stats"][:, 15:16], 1.0 / 127.0, ALU.mult,
                    1e-8, ALU.max))
            vop(tt_(b["gam"][:, t8:t8 + 1], b["stats"][:, 9:10], b["perm"][:, 0:1],
                    ALU.mult))
            vop(lambda e: e.reciprocal(b["stats"][:, 10:11], b["stats"][:, 9:10]))
            vop(tt_(b["stats"][:, 11:12], b["stats"][:, 7:8], b["stats"][:, 10:11],
                    ALU.mult))
            vop(stt_(b["stats"][:, 8:9], b["stats"][:, 2:3], -1.0, b["stats"][:, 11:12],
                     ALU.mult, ALU.mult))
            vop(ts_(b["stats"][:, 8:9], b["stats"][:, 8:9], 128.0, ALU.add))
            t_w6 = vop(ts_(b["w6"][:], b["out1"][:], b["stats"][:, 11:12],
                           ALU.mult, b["stats"][:, 8:9], ALU.add))
            sq_prev = ladder_v(b["w6"][:], b["r6"][:], b["fin"][:], 8)
            vop(ts_(b["xq"][:, t8, :], b["r6"][:], -128.0, ALU.add))
            p.w("sync", "vS", pb_free_wo)
            t_st = p.dma("sync", "dX", d["out1_scr"][t8 * 128:(t8 + 1) * 128, :],
                         b["out1"][:])
            out1_free = (t_w6, out1_sq, t_st)
        p.fence()

        ph1(f"i{it}_wo")
        if KPH < 7:
            p.fence()
            continue
        ph0(f"i{it}_ffn")
        if it == 0:
            wf2_free = [0, 0]
            for ic in range(16):
                bi = ic % 2
                p.w("sync", "vS", wf2_free[bi])
                t_wf = p.dma("sync", "dX", b["wf2"][:, bi, :], d["WfT"][ic])
                p.w("vector", "dX", t_wf)
                vop(ts_(b["out1r"][:, 0, :], b["wf2"][:, bi, :], b["perm"][:, 2:3],
                        ALU.mult, 0.5, ALU.is_gt))
                vop(ts_(b["out1r"][:, 1, :], b["wf2"][:, bi, :], b["perm"][:, 2:3],
                        ALU.mult, -0.5, ALU.is_lt))
                wf2_free[bi] = vop(tt_(b["wbig"][:, ic, :], b["out1r"][:, 0, :],
                                       b["out1r"][:, 1, :], ALU.subtract))
            wq_ready_v = p.cnt["vS"]
            p.w("sync", "vS", wq_ready_v)
            t_wq = 0
            p.dma("sync", "dX", d["wq_hbm"][:], b["wbig"][:])
        else:
            wq_ready_v = 0
            t_wq = p.dma("sync", "dX", b["wbig"][:], d["wq_hbm"][:])
        tr_list = []
        for t8 in range(8):
            tr_list.append(p.dmaT(b["xqT"][:, :, t8 * 128:(t8 + 1) * 128],
                                  b["xq"][:, t8, :]))
        fin_last = 0
        o1_free = [0, 0]
        st_free = [0, 0]
        for t8 in range(8):
            bi = t8 % 2
            p.w("sync", "vS", o1_free[bi])
            t_o1 = p.dma("sync", "dX", b["out1r"][:, bi, :],
                         d["out1_scr"][t8 * 128:(t8 + 1) * 128, :])
            p.w("tensor", "dT", tr_list[t8])
            p.w("tensor", "dX", t_wq)
            p.w("tensor", "vS", wq_ready_v)
            p.w("tensor", "vS", fin_last)
            t_f = 0
            for ic in range(16):
                for n in range(4):
                    t_f = p.mm(mm_(pb[n][:, :], b["xqT"][:, ic, t8 * 128:(t8 + 1) * 128],
                                   b["wbig"][:, ic, n * 512:(n + 1) * 512],
                                   ic == 0, ic == 15),
                               tok=(ic == 15 and n == 3))
            p.w("vector", "tS", t_f)
            p.w("vector", "dX", t_o1)
            p.w("vector", "dT", tr_list[-1])
            p.w("vector", "dO", st_free[bi])
            for n in range(4):
                t_fin = vop(stt_(b["fin2"][:, bi, n * 512:(n + 1) * 512], pb[n][:, :],
                                 b["gam"][:, t8:t8 + 1],
                                 b["out1r"][:, bi, n * 512:(n + 1) * 512],
                                 ALU.mult, ALU.add))
            fin_last = t_fin
            o1_free[bi] = t_fin
            p.w("sync", "vS", t_fin)
            st_free[bi] = p.dma("sync", "dO", d["out"][it, t8 * 128:(t8 + 1) * 128, :],
                                b["fin2"][:, bi, :])
        p.fence()
        ph1(f"i{it}_ffn")

    p.w("sync", "dO", p.cnt["dO"])



def kernel(**inputs):
    in_maps = stage_inputs(inputs)
    ktabs = build_program(None)
    nc, es = build_program(ktabs)
    try:
        res = run_bass_kernel_spmd(nc, in_maps, core_ids=list(range(8)))
    finally:
        es.close()
    return gather_output(res.results)


# version 16 (final)
# speedup vs baseline: 1.1454x; 1.1454x over previous
import math
from contextlib import ExitStack

import numpy as np
import ml_dtypes

import concourse.bass as bass
import concourse.mybir as mybir
from concourse.ap import AP as RawAP
from concourse.bass_utils import run_bass_kernel_spmd

F32 = mybir.dt.float32
BF16 = mybir.dt.bfloat16
AF = mybir.ActivationFunctionType
ALU = mybir.AluOpType
AX = mybir.AxisListType

B, L, H = 2, 8192, 2048
NH, HD = 32, 64
QB, LB = 1024, 512
SPAN = QB + LB
NSC = SPAN // 128
P_PAD = (2560, 6656)
NPRE = (P_PAD[0] // 128, P_PAD[1] // 128)
NPT = (P_PAD[0] // 512, P_PAD[1] // 512)
TWO_PI = 6.283185307179586
CW_C1 = 6.28125
CW_C2 = TWO_PI - CW_C1
LN10K_32 = math.log(10000.0) / 32.0
NEGBIG = -30000.0

V_FLAT = 12 * 2080
VG_FLAT = 52 * 128


def _core_blocks(c):
    jlo, jhi = min(c, 7 - c), max(c, 7 - c)
    b0 = 0 if c <= 3 else 1
    return [(b0, jlo), (1 - b0, jhi)]



def _tile_wT(W):
    t = np.ascontiguousarray(W.T).reshape(16, 128, 16, 128)
    t = t.transpose(2, 1, 0, 3)
    t = t.reshape(8, 2, 128, 16, 128).transpose(0, 2, 1, 3, 4)
    return np.ascontiguousarray(t)


def stage_inputs(inputs):
    x = np.ascontiguousarray(inputs["L0_attn_hidden_states"], dtype=np.float32)
    WqT = _tile_wT(np.asarray(inputs["L0_attn_Wq"], np.float32))
    WkT = _tile_wT(np.asarray(inputs["L0_attn_Wk"], np.float32))
    WvT = _tile_wT(np.asarray(inputs["L0_attn_Wv"], np.float32))
    WoT = np.ascontiguousarray(np.asarray(inputs["L0_attn_Wo"], np.float32)
                               .T.reshape(16, 128, 2048))
    WfT = np.ascontiguousarray(np.asarray(inputs["L0_ffn_weight"], np.float32)
                               .T.reshape(16, 128, 2048))
    bo = np.ascontiguousarray(np.asarray(inputs["L0_attn_bo"], np.float32)
                              .reshape(1, 2048))

    prot = np.zeros((128, 128), np.float32)
    for blk in range(2):
        for m in range(64):
            perm = m + 32 if m < 32 else m - 32
            prot[blk * 64 + perm, blk * 64 + m] = 1.0
    diag = np.triu(np.ones((128, 128), np.float32))
    sgn = np.where(np.arange(128) % 64 < 32, -1.0, 1.0).astype(np.float32).reshape(128, 1)
    ident = np.eye(128, dtype=np.float32)

    in_maps = []
    for c in range(8):
        x_span = np.zeros((2, SPAN, H), np.float32)
        x_spanT = np.zeros((2, 128, 16, SPAN), np.float32)
        kvn_span = np.full((2, 128, NSC), NEGBIG, np.float32)
        posb = np.zeros((2, 64), np.float32)
        xp = [np.zeros((128, 16, P_PAD[0]), np.float32),
              np.zeros((128, 16, P_PAD[1]), np.float32)]
        kvn_pre = [np.full((128, NPRE[0]), NEGBIG, np.float32),
                   np.full((128, NPRE[1]), NEGBIG, np.float32)]
        for it, (bb, j) in enumerate(_core_blocks(c)):
            s = j * QB
            lo = max(0, s - LB)
            off = lo - (s - LB)
            x_span[it, off:] = x[bb, lo:s + QB]
            x_spanT[it, :, :, off:] = (x[bb, lo:s + QB].T
                                       .reshape(16, 128, SPAN - off)
                                       .transpose(1, 0, 2))
            kv = np.full(SPAN, NEGBIG, np.float32)
            kv[off:] = 0.0
            kvn_span[it] = kv.reshape(NSC, 128).T
            pos = np.zeros(SPAN, np.float32)
            pos[off:] = np.arange(lo, s + QB, dtype=np.float32)
            posb[it, :NSC] = pos[::128]
            plen = lo
            xp[it][:, :, :plen] = (x[bb, :plen].T
                                   .reshape(16, 128, plen).transpose(1, 0, 2))
            kvp = np.full(P_PAD[it], NEGBIG, np.float32)
            kvp[:plen] = 0.0
            kvn_pre[it] = kvp.reshape(NPRE[it], 128).T
            pp = np.zeros(P_PAD[it], np.float32)
            pp[:plen] = np.arange(plen, dtype=np.float32)
            posb[it, NSC:NSC + NPRE[it]] = pp[::128]
        in_maps.append({
            "x_span": x_span, "x_spanT": x_spanT,
            "x_pre_lo": xp[0], "x_pre_hi": xp[1],
            "kvn_span": np.ascontiguousarray(kvn_span),
            "kvn_pre_lo": kvn_pre[0], "kvn_pre_hi": kvn_pre[1],
            "posb": posb,
            "WqT": WqT, "WkT": WkT, "WvT": WvT, "WoT": WoT, "WfT": WfT,
            "bo": bo, "prot": prot, "diag": diag, "sgn": sgn, "ident": ident,
        })
    return in_maps


def gather_output(results):
    out = np.zeros((B, L, H), np.float32)
    for c in range(8):
        o = np.asarray(results[c]["out"], np.float32)
        for it, (bb, j) in enumerate(_core_blocks(c)):
            out[bb, j * QB:(j + 1) * QB] = o[it]
    return out



class Arena:
    def __init__(self, nc, size_bytes):
        self.nc, self.size, self.items = nc, size_bytes, []

    def at(self, name, shape, dtype, off):
        per = int(np.prod(shape[1:])) * mybir.dt.size(dtype)
        assert off % 32 == 0 and off + per <= self.size, (name, off, per)
        h = self.nc.virtual_sbuf_tensor(name, list(shape), dtype)
        self.items.append((h, off))
        return h

    def commit(self, es):
        big = es.enter_context(
            self.nc.sbuf_tensor("arena", [128, self.size], mybir.dt.uint8))
        base = self.nc.lookup_mloc(big).addr
        for h, off in self.items:
            self.nc._tensor(h.name, list(h.shape), h.dtype, type="SB")
            mloc = self.nc.lookup_mloc(h)
            mloc.allocated = not self.nc.target_bir_lowering
            mloc.addr = base + off
        return big


A_K = 0
A_V = 49_152
A_Q = 99_072
A_COSP, A_SINP = 0, 13_312
A_WBIG = 0
A_W6, A_R6 = 65_536, 73_728
A_OUT1 = 81_920
A_FIN = 90_112
A_XQ = 98_432
A_FIN2 = 114_688
A_XQT = 65_536
O_KG = 131_968
O_VG = 145_280
O_WF = 131_968
O_BO = 140_160
O_OUT1R = 131_968
O_WF2 = 148_352
O_ATTNT = 158_720
O_XT = 158_720
O_XN = 162_816
O_XNT = 166_912
O_WSL = 183_296
O_XTB = 191_488
O_XNB = 195_584
O_BIAS_S = 195_584
O_BIAS_P = 197_120
O_ELOR = 197_536
O_EHIR = 198_048
O_ONESR = 198_560
O_DIAGB = 199_072
O_ONESB = 199_328
O_AB = 191_488
O_ROPE = 199_936
O_KU, O_T1R, O_T2R, O_VFM = 199_936, 200_960, 201_984, 203_008
O_THA, O_THB, O_THG = 199_936, 200_192, 200_448
O_IVF, O_RROW, O_POSR = 200_704, 201_216, 201_728
O_RECIPS = 199_936
O_FACT = 201_984
O_AU = 203_008
O_CSC = 204_032
O_CSS = 205_056
O_T1C, O_T1S, O_T1NS = 206_080, 206_336, 206_592
O_CG = 206_848
O_T2C, O_T2S = 207_360, 207_872
O_STATS = 208_384
O_NRM = 208_512
O_SC1 = 208_768
O_GAM = 209_024
O_PERM = 209_056
O_SQN = 209_088
O_PROTB = 210_112
O_IDENT = 210_368
O_BPI = 210_624
O_ZC = 210_656
O_EPSC = 210_688
O_SGNC = 210_720
O_ELOC = 210_752
O_EHIC = 210_784
O_KVS = 210_816
O_KVP = 210_912
O_VONES = 211_136
O_SWK = 211_168
O_SWV = 211_424
O_SQB = 191_488
O_ONES32 = 211_680
ARENA_SZ = 212_192


SEM_NAMES = ["dX", "dT", "dW", "dO", "tS", "vS", "sS", "gS"]
SERIAL = {"vector": "vS", "scalar": "sS", "gpsimd": "gS"}


class Prog:

    def __init__(self, nc, bufs, sems, eng_name, engine, ktabs=None):
        self.nc, self.b, self.sems = nc, bufs, sems
        self.cnt = {k: 0 for k in SEM_NAMES}
        self.eng_name, self.e = eng_name, engine
        self.ktabs = ktabs
        self.kidx = {"sync": 0, "gpsimd": 0}
        self.krec = {"sync": [], "gpsimd": []}

    def on(self, owner):
        return self.eng_name == owner

    def w(self, owner, sem, val):
        if self.on(owner) and val > 0:
            self.e.wait_ge(self.sems[sem], val)

    def op(self, owner, fn, n=1):
        sem = SERIAL[owner]
        pre = self.cnt[sem]
        tok = self.cnt[sem] = pre + n
        if self.on(owner):
            if pre > 0:
                self.e.wait_ge(self.sems[sem], pre)
            inst = fn(self.e)
            inst.then_inc(self.sems[sem], n)
        return tok

    def mm(self, fn, tok=False):
        if tok:
            t = self.cnt["tS"] = self.cnt["tS"] + 1
            if self.on("tensor"):
                fn(self.e).then_inc(self.sems["tS"], 1)
            return t
        if self.on("tensor"):
            fn(self.e)
        return None

    def _dma_k(self, owner, emit):
        if self.ktabs is None:
            k = 1
            if self.on(owner):
                n0 = self.nc.n_instructions()
                emit()
                k = self.nc.n_instructions() - n0
                self.krec[owner].append(k)
            return k
        k = self.ktabs[owner][self.kidx[owner]]
        self.kidx[owner] += 1
        if self.on(owner):
            emit()
        return k

    def dma(self, owner, sem, out, in0):
        k = self._dma_k(owner, lambda: self.e.dma_start(out=out, in_=in0)
                        .then_inc(self.sems[sem], 16))
        tok = self.cnt[sem] = self.cnt[sem] + 16 * k
        if self.on(owner):
            self.e.wait_ge(self.sems[sem], tok)
        return tok

    def dmaT(self, out, in0):
        k = self._dma_k("sync", lambda: self.e.dma_start_transpose(out, in0)
                        .then_inc(self.sems["dT"], 16))
        tok = self.cnt["dT"] = self.cnt["dT"] + 16 * k
        if self.on("sync"):
            self.e.wait_ge(self.sems["dT"], tok)
        return tok

    def fence(self):
        snap = dict(self.cnt)
        own = {"vector": "vS", "scalar": "sS", "gpsimd": "gS", "tensor": "tS"}.get(self.eng_name)
        for sem, val in snap.items():
            if sem != own and val > 0:
                if self.e is not None:
                    self.e.wait_ge(self.sems[sem], val)
        return snap


def build_program(ktabs=None):
    nc = bass.Bass("TRN2", target_bir_lowering=False)
    dram = {}

    def din(name, shape, dtype=F32):
        dram[name] = nc.dram_tensor(name, list(shape), dtype, kind="ExternalInput").ap()

    din("x_span", [2, SPAN, H]); din("x_spanT", [2, 128, 16, SPAN])
    din("x_pre_lo", [128, 16, P_PAD[0]]); din("x_pre_hi", [128, 16, P_PAD[1]])
    din("kvn_span", [2, 128, NSC]); din("kvn_pre_lo", [128, NPRE[0]]); din("kvn_pre_hi", [128, NPRE[1]])
    din("posb", [2, 64])
    din("WqT", [8, 128, 2, 16, 128]); din("WkT", [8, 128, 2, 16, 128]); din("WvT", [8, 128, 2, 16, 128])
    din("WoT", [16, 128, 2048]); din("WfT", [16, 128, 2048])
    din("bo", [1, 2048]); din("prot", [128, 128]); din("diag", [128, 128]); din("sgn", [128, 1])
    din("ident", [128, 128])
    dram["out"] = nc.dram_tensor("out", [2, QB, H], F32, kind="ExternalOutput").ap()
    dram["out1_scr"] = nc.dram_tensor("out1_scr", [QB, H], F32, kind="Internal").ap()
    dram["wq_hbm"] = nc.dram_tensor("wq_hbm", [128, 16, 2048], BF16, kind="Internal").ap()
    dram["wb_hbm"] = nc.dram_tensor("wb_hbm", [3, 8, 128, 2, 16, 128], BF16,
                                    kind="Internal").ap()

    es = ExitStack()
    ar = Arena(nc, ARENA_SZ)
    bufs = {"dram": dram}

    def ab(name, shape, dtype, off):
        bufs[name] = ar.at(name, shape, dtype, off)

    ab("K", [128, 16, SPAN], BF16, A_K)
    ab("V", [128, V_FLAT], BF16, A_V)
    ab("Q", [128, 16, QB], BF16, A_Q)
    ab("cosp", [128, P_PAD[1]], BF16, A_COSP)
    ab("sinp", [128, P_PAD[1]], BF16, A_SINP)
    ab("wbig", [128, 16, 2048], BF16, A_WBIG)
    ab("w6", [128, 2048], F32, A_W6)
    ab("r6", [128, 2048], F32, A_R6)
    ab("out1", [128, 2048], F32, A_OUT1)
    ab("xq", [128, 8, 2048], BF16, A_XQ)
    ab("fin", [128, 2048], F32, A_FIN)
    ab("fin2", [128, 2, 2048], F32, A_FIN2)
    ab("xqT", [128, 16, QB], BF16, A_XQT)
    ab("Kg", [128, P_PAD[1]], BF16, O_KG)
    ab("Vg", [128, VG_FLAT], BF16, O_VG)
    ab("wf", [128, 2048], F32, O_WF)
    ab("bo_sb", [1, 2048], BF16, O_BO)
    ab("out1r", [128, 2, 2048], F32, O_OUT1R)
    ab("wf2", [128, 2, 2048], F32, O_WF2)
    ab("attnT", [128, 16, QB], BF16, O_ATTNT)
    ab("sqs", [128, 2, 512], BF16, O_XT)
    ab("rows", [33, 512], F32, O_XT + 2_048)
    ab("rows2", [33, 512], F32, O_XT + 4_096)
    ab("kuB", [128, 512], BF16, O_XT + 6_144)
    ab("sqnB", [128, 512], BF16, O_XT + 7_168)
    ab("wfb", [128, 2048], F32, O_XT)
    ab("sqb", [128, 8, 512], BF16, O_SQB)
    ab("swk", [1, 128], BF16, O_SWK)
    ab("swv", [1, 128], BF16, O_SWV)
    ab("onesP", [1, 128], F32, O_ONES32)
    ab("xnT", [128, 16, 512], BF16, O_XNT)
    ab("wsl", [128, 2, 2, 16, 128], BF16, O_WSL)
    ab("bias_s", [128, 32, NSC], F32, O_BIAS_S)
    ab("bias_p", [128, 2, NPRE[1]], F32, O_BIAS_P)
    ab("kvs", [128, 2, NSC], F32, O_KVS)
    ab("kvp", [128, NPRE[1]], F32, O_KVP)
    ab("vones", [128, 1], BF16, O_VONES)
    ab("eloR", [33, 128], F32, O_ELOR)
    ab("onesr", [1, 128], F32, O_ONESR)
    ab("onesb", [1, 128], BF16, O_ONESB)
    ab("diag_b", [128, 128], BF16, O_DIAGB)
    ab("Ab", [128, 4, 512], BF16, O_AB)
    ab("sqn", [128, 512], BF16, O_SQN)
    ab("prot_b", [128, 128], BF16, O_PROTB)
    ab("ident_b", [128, 128], BF16, O_IDENT)
    ab("bpi", [128, 1], F32, O_BPI)
    ab("zc", [128, 1], F32, O_ZC)
    ab("epsc", [128, 1], F32, O_EPSC)
    ab("sgn_c", [128, 1], F32, O_SGNC)
    ab("eloC", [128, 1], BF16, O_ELOC)
    ab("ehiC", [128, 1], BF16, O_EHIC)
    ab("ku", [128, 512], BF16, O_KU)
    ab("t1r", [128, 512], BF16, O_T1R)
    ab("t2r", [128, 512], BF16, O_T2R)
    ab("vfm", [128, 512], BF16, O_VFM)
    ab("tha", [128, 64], F32, O_THA)
    ab("thb", [128, 64], F32, O_THB)
    ab("thg", [128, 64], F32, O_THG)
    ab("ivf", [1, 128], F32, O_IVF)
    ab("rrow", [1, 128], F32, O_RROW)
    ab("posr", [1, 64], F32, O_POSR)
    ab("recips", [33, 512], F32, O_RECIPS)
    ab("fact", [128, 512], BF16, O_FACT)
    ab("attnU", [128, 512], BF16, O_AU)
    ab("cs_c", [128, 512], BF16, O_CSC)
    ab("cs_s", [128, 512], BF16, O_CSS)
    ab("T1c", [128, 64], F32, O_T1C)
    ab("T1s", [128, 64], F32, O_T1S)
    ab("T1ns", [128, 64], F32, O_T1NS)
    ab("cg", [128, 128], F32, O_CG)
    ab("T2c", [128, 128], F32, O_T2C)
    ab("T2s", [128, 128], F32, O_T2S)
    ab("stats", [128, 32], F32, O_STATS)
    ab("nrm", [1, 64], F32, O_NRM)
    ab("sc1", [1, 64], F32, O_SC1)
    ab("gam", [128, 8], F32, O_GAM)
    ab("perm", [128, 8], F32, O_PERM)

    ar.commit(es)
    bufs["pb"] = [es.enter_context(nc.psum_tensor(f"pb{i}", [128, 512], F32))
                  for i in range(8)]
    sems = {k: es.enter_context(nc.semaphore(k)) for k in SEM_NAMES}
    block = es.enter_context(nc.Block())

    counts = {}
    krec = {}

    def section(eng_name):
        def run(e):
            p = Prog(nc, bufs, sems, eng_name, e, ktabs)
            walk(p)
            counts[eng_name] = dict(p.cnt)
            if ktabs is None and eng_name in ("sync", "gpsimd"):
                krec[eng_name] = p.krec[eng_name]
        return run

    block.sync(section("sync"))
    block.tensor(section("tensor"))
    block.scalar(section("scalar"))
    block.vector(section("vector"))
    block.gpsimd(section("gpsimd"))
    if ktabs is None:
        es.close()
        return krec
    ref = counts["sync"]
    for k, v in counts.items():
        assert v == ref, (k, v, ref)
    return nc, es



def walk(p: Prog):
    import os
    KPH = int(os.environ.get("KPH", "9"))
    KIT = int(os.environ.get("KIT", "2"))
    _scopes = {}

    def ph0(name):
        if KPH >= 9:
            _scopes[name] = p.nc.enter_named_scope(name, False)[0]

    def ph1(name):
        if KPH >= 9:
            p.nc.leave_named_scope(name, _scopes.pop(name), False)
    b = p.b
    d = b["dram"]
    pb = b["pb"]

    def vop(fn, n=1):
        return p.op("vector", fn, n)

    def sop(fn, n=1):
        return p.op("scalar", fn, n)

    def gop(fn, n=1):
        return p.op("gpsimd", fn, n)

    def ts_(out, in0, s1, op0, s2=None, op1=None):
        if s2 is None and op1 is None:
            return lambda e: e.tensor_single_scalar(out, in0, s1, op0)
        return lambda e: e.tensor_scalar(out, in0, s1, s2, op0, op1)

    def stt_(out, in0, s, in1, op0, op1):
        return lambda e: e.scalar_tensor_tensor(out, in0, s, in1, op0, op1)

    def tt_(out, in0, in1, op):
        return lambda e: e.tensor_tensor(out, in0, in1, op)

    def act_(out, in0, func, bias, scale=1.0, accum=None):
        return lambda e: e.activation(out, in0, func, bias=bias, scale=scale,
                                      accum_out=accum)

    def red_(out, in0, op, absval=False):
        return lambda e: e.tensor_reduce(out, in0, axis=AX.X, op=op,
                                         apply_absolute_value=absval)

    def mm_(out, lhsT, rhs, start, stop):
        return lambda e: e.matmul(out, lhsT, rhs, start=start, stop=stop,
                                  skip_group_check=True)

    def ladder_v(src, dst, tmp, levels):
        vop(ts_(tmp, src, float(2 ** (levels - 1)) - 0.5, ALU.is_ge))
        vop(ts_(dst, tmp, float(2 ** (levels - 1)), ALU.mult))
        for i in range(levels - 2, -1, -1):
            vop(stt_(tmp, dst, float(2 ** i) - 0.5, src, ALU.add, ALU.is_le))
            t = vop(stt_(dst, tmp, float(2 ** i), dst, ALU.mult, ALU.add))
        return t

    t_sgn = p.dma("sync", "dX", b["sgn_c"][:], d["sgn"][:])
    gop(lambda e: e.memset(b["perm"][:], 0.0))
    gop(lambda e: e.memset(b["vones"][:], 1.0))
    p.dma("gpsimd", "dW", b["ident_b"][:], d["ident"][:])
    p.fence()

    for it in range(KIT):
        npre, npt, ppad = NPRE[it], NPT[it], P_PAD[it]
        x_pre = d["x_pre_lo"] if it == 0 else d["x_pre_hi"]
        kvn_pre = d["kvn_pre_lo"] if it == 0 else d["kvn_pre_hi"]

        ph0(f"i{it}_tab")
        t_prot = p.dma("gpsimd", "dW", b["prot_b"][:], d["prot"][:])
        gop(lambda e: e.memset(b["bpi"][:], 1.5707963267948966))
        gop(lambda e: e.memset(b["zc"][:], 0.0))
        gop(lambda e: e.memset(b["epsc"][:], 1e-5))
        gop(lambda e: e.memset(b["onesP"][0:1, :], 1.0))
        gop(lambda e: e.memset(b["eloC"][0:64, :], 1.0))
        gop(lambda e: e.memset(b["eloC"][64:128, :], 0.0))
        gop(lambda e: e.memset(b["ehiC"][0:64, :], 0.0))
        gop(lambda e: e.memset(b["ehiC"][64:128, :], 1.0))
        gop(lambda e: e.memset(b["nrm"][:], 0.0))
        t_pos = p.dma("sync", "dX", b["posr"][:], d["posb"][it:it + 1, 0:64])

        gop(lambda e: e.iota(b["rrow"][:], pattern=[[0, 4], [1, 32]], base=0, channel_multiplier=0,
                             allow_small_or_imprecise_dtypes=True))
        t = p.cnt["gS"]
        p.w("scalar", "gS", t)
        sop(act_(b["ivf"][:], b["rrow"][:], AF.Exp, b["zc"][0:1, :], -LN10K_32))
        t = p.cnt["sS"]
        p.w("gpsimd", "sS", t)
        gop(lambda e: e.iota(b["rrow"][:], pattern=[[1, 128]], base=0, channel_multiplier=0,
                             allow_small_or_imprecise_dtypes=True))
        p.w("tensor", "gS", p.cnt["gS"])
        p.w("tensor", "sS", p.cnt["sS"])
        t_mm = p.mm(mm_(pb[6][:, 0:128], b["ivf"][:], b["rrow"][:], True, True), tok=True)
        p.w("vector", "tS", t_mm)

        tabst = {"sin_done": 0}

        def sincos_from_theta(th_psum, w, dc, ds, k_sc, lad_lv):
            p.w("vector", "sS", tabst["sin_done"])
            vop(ts_(b["tha"][:, 0:w], th_psum, 1.0 / TWO_PI, ALU.mult))
            ladder_v(b["tha"][:, 0:w], b["thb"][:, 0:w], b["thg"][:, 0:w], lad_lv)
            vop(stt_(b["tha"][:, 0:w], b["thb"][:, 0:w], -CW_C1, th_psum,
                     ALU.mult, ALU.add))
            vop(stt_(b["tha"][:, 0:w], b["thb"][:, 0:w], -CW_C2,
                     b["tha"][:, 0:w], ALU.mult, ALU.add))
            vop(ts_(b["thg"][:, 0:w], b["tha"][:, 0:w], math.pi / 2, ALU.add))
            vop(ts_(b["thb"][:, 0:w], b["thg"][:, 0:w], math.pi, ALU.is_gt))
            t = vop(stt_(b["thg"][:, 0:w], b["thb"][:, 0:w], -TWO_PI,
                         b["thg"][:, 0:w], ALU.mult, ALU.add))
            p.w("scalar", "vS", t)
            sop(act_(dc, b["thg"][:, 0:w], AF.Sin, b["zc"][:, 0:1]))
            tok = sop(act_(ds, b["tha"][:, 0:w], AF.Sin, b["zc"][:, 0:1]))
            tabst["sin_done"] = tok
            return tok

        t_s = sincos_from_theta(pb[6][:, 0:64], 64, b["T2c"][:, 0:64],
                                b["T2s"][:, 0:64], None, 5)
        t_s = sincos_from_theta(pb[6][:, 64:128], 64, b["T2c"][:, 64:128],
                                b["T2s"][:, 64:128], None, 5)
        p.w("tensor", "dX", t_pos)
        p.w("tensor", "sS", t_s)
        p.w("tensor", "vS", p.cnt["vS"])
        t_mm = p.mm(mm_(pb[6][:, 0:64], b["ivf"][:], b["posr"][:], True, True), tok=True)
        p.w("vector", "tS", t_mm)
        t_s = sincos_from_theta(pb[6][:, 0:64], 64, b["T1c"][:, 0:64],
                                b["T1s"][:, 0:64], None, 11)
        p.w("vector", "sS", t_s)
        vop(ts_(b["T1ns"][:, 0:64], b["T1s"][:, 0:64], -1.0, ALU.mult))

        def cosgen(col, dstc, dsts):
            vop(ts_(b["cg"][:], b["T2c"][:], b["T1c"][:, col:col + 1], ALU.mult))
            vop(stt_(dstc, b["T2s"][:], b["T1ns"][:, col:col + 1], b["cg"][:],
                     ALU.mult, ALU.add))
            vop(ts_(b["cg"][:], b["T2s"][:], b["T1c"][:, col:col + 1], ALU.mult))
            vop(stt_(b["cg"][:], b["T2c"][:], b["T1s"][:, col:col + 1], b["cg"][:],
                     ALU.mult, ALU.add))
            return vop(ts_(dsts, b["cg"][:], b["sgn_c"][:, 0:1], ALU.mult))

        p.w("vector", "dX", t_sgn)
        for bi in range(npre):
            cosgen(NSC + bi, b["cosp"][:, bi * 128:(bi + 1) * 128],
                   b["sinp"][:, bi * 128:(bi + 1) * 128])
        p.fence()

        state = {"xn_tr": 0, "xn_tr_last": 0, "proj_done": 0,
                 "sq_done": 0, "red_done": 0, "sq_junk": 0, "ts3": 0}

        def ln_subtile(src_row_ap, col0):
            xt, xn = b["xt"], b["xn"]
            p.w("gpsimd", "vS", max(state["sq_done"], state["red_done"]))
            p.w("gpsimd", "sS", state["sq_junk"])
            t_x = p.dma("gpsimd", "dW", xt[:], src_row_ap)
            p.w("vector", "dW", t_x)
            state["red_done"] = vop(red_(b["stats"][:, 0:1], xt[:], ALU.add))
            vop(ts_(b["stats"][:, 2:3], b["stats"][:, 0:1], 1.0 / 2048.0, ALU.mult))
            p.w("scalar", "dW", t_x)
            p.w("scalar", "dT", state["xn_tr"])
            p.w("scalar", "vS", state["ts3"])
            t_sq = sop(act_(xn[:], xt[:], AF.Square, b["zc"][:, 0:1], 1.0,
                            b["stats"][:, 1:2]))
            state["sq_junk"] = t_sq
            p.w("vector", "sS", t_sq)
            state["ts3"] = vop(ts_(b["stats"][:, 3:4], b["stats"][:, 1:2],
                                   1.0 / 2048.0, ALU.mult))
            vop(tt_(b["stats"][:, 4:5], b["stats"][:, 2:3], b["stats"][:, 2:3], ALU.mult))
            t_v = vop(tt_(b["stats"][:, 5:6], b["stats"][:, 3:4], b["stats"][:, 4:5],
                          ALU.subtract))
            p.w("scalar", "vS", t_v)
            t_s = sop(act_(b["stats"][:, 6:7], b["stats"][:, 5:6], AF.Sqrt,
                           b["epsc"][:, 0:1]))
            p.w("vector", "sS", t_s)
            vop(lambda e: e.reciprocal(b["stats"][:, 7:8], b["stats"][:, 6:7]))
            vop(stt_(b["stats"][:, 8:9], b["stats"][:, 2:3], -1.0,
                     b["stats"][:, 7:8], ALU.mult, ALU.mult))
            state["sq_done"] = vop(ts_(xn[:], xt[:], b["stats"][:, 7:8],
                                       ALU.mult, b["stats"][:, 8:9], ALU.add))
            p.w("sync", "vS", state["sq_done"])
            p.w("sync", "tS", state["proj_done"])
            state["xn_tr"] = p.dmaT(b["xnT"][:, :, col0:col0 + 128], xn[:])
            state["xn_tr_last"] = state["xn_tr"]
            return state["xn_tr_last"]

        ph1(f"i{it}_tab")
        if KPH < 2:
            p.fence()
            continue
        ph0(f"i{it}_pre")
        w15_sem = "dW" if it == 0 else "dX"
        if it == 0:
            t_w15 = p.dma("gpsimd", "dW", b["wsl"][:, 0, 0], d["WkT"][7, :, 1])
            t_w15 = p.dma("gpsimd", "dW", b["wsl"][:, 0, 1], d["WvT"][7, :, 1])
        else:
            t_w15 = p.dma("sync", "dX", b["wsl"][:, 0, 0], d["wb_hbm"][0, 7][:, 1])
            t_w15 = p.dma("sync", "dX", b["wsl"][:, 0, 1], d["wb_hbm"][1, 7][:, 1])
        pstate = {"pb_free": [0, 0], "rot_free": [0, 0], "ku_read": [0, 0],
                  "sqn_free": [0, 0], "nrm_free": 0, "vfm_tr": 0}

        def norms(dst_slice, qcol0, t_ready, w=512, sq=0):
            sqn = b["sqnB"] if sq else b["sqn"]
            p.w("gpsimd", "tS", pstate["sqn_free"][sq])
            p.w("gpsimd", "vS", t_ready)
            t_sq = gop(tt_(sqn[:, 0:w], dst_slice, dst_slice, ALU.mult))
            p.w("tensor", "gS", t_sq)
            p.w("tensor", "vS", pstate["nrm_free"])
            p.mm(mm_(pb[7][0:1, 0:w], b["eloC"][:], sqn[:, 0:w], True, True))
            t_nm = p.mm(mm_(pb[6][0:1, 0:w], b["ehiC"][:], sqn[:, 0:w],
                            True, True), tok=True)
            pstate["sqn_free"][sq] = t_nm
            p.w("vector", "tS", t_nm)
            vop(red_(b["sc1"][0:1, 0:1], pb[7][0:1, 0:w], ALU.max))
            vop(tt_(b["nrm"][0:1, qcol0:qcol0 + 1], b["nrm"][0:1, qcol0:qcol0 + 1],
                    b["sc1"][0:1, 0:1], ALU.max))
            vop(red_(b["sc1"][0:1, 0:1], pb[6][0:1, 0:w], ALU.max))
            pstate["nrm_free"] = vop(tt_(
                b["nrm"][0:1, qcol0 + 1:qcol0 + 2], b["nrm"][0:1, qcol0 + 1:qcol0 + 2],
                b["sc1"][0:1, 0:1], ALU.max))

        def rope_evac(pp, cos_sl, sin_sl, dst_slice, w=256, rp=0, rstd_ps=None,
                      kb=0):
            ku = b["kuB"] if kb else b["ku"]
            p.w("scalar", "vS", pstate["ku_read"][kb])
            t_ku = sop(lambda e: e.copy(ku[:, 0:w], pb[pp][:, 0:w]))
            pstate["pb_free"][pp] = t_ku
            p.w("tensor", "sS", t_ku)
            p.w("tensor", "vS", pstate["rot_free"][rp])
            t_rot = p.mm(mm_(pb[2 + rp][:, 0:w], b["prot_b"][:], ku[:, 0:w],
                             True, True), tok=True)
            p.w("vector", "sS", t_ku)
            pstate["ku_read"][kb] = vop(tt_(b["t1r"][:, 0:w], ku[:, 0:w], cos_sl,
                                            ALU.mult))
            p.w("vector", "tS", t_rot)
            pstate["rot_free"][rp] = vop(tt_(b["t2r"][:, 0:w], pb[2 + rp][:, 0:w],
                                             sin_sl, ALU.mult))
            if rstd_ps is None:
                return vop(tt_(dst_slice, b["t1r"][:, 0:w], b["t2r"][:, 0:w],
                               ALU.add))
            vop(tt_(b["t1r"][:, 0:w], b["t1r"][:, 0:w], b["t2r"][:, 0:w], ALU.add))
            return vop(tt_(dst_slice, b["t1r"][:, 0:w], rstd_ps[:, 0:w], ALU.mult))

        p.w("tensor", w15_sem, t_w15)
        t = 0
        for hc in range(16):
            t = p.mm(mm_(pb[4][0:1, 0:128], b["vones"][:], b["wsl"][:, 0, 0, hc, :],
                         hc == 0, hc == 15), tok=(hc == 15))
        p.w("scalar", "tS", t)
        t_evk = sop(lambda e: e.copy(b["swk"][:], pb[4][0:1, 0:128]))
        p.w("tensor", "sS", t_evk)
        for hc in range(16):
            t = p.mm(mm_(pb[4][0:1, 0:128], b["vones"][:], b["wsl"][:, 0, 1, hc, :],
                         hc == 0, hc == 15), tok=(hc == 15))
        p.w("scalar", "tS", t)
        t_evv = sop(lambda e: e.copy(b["swv"][:], pb[4][0:1, 0:128]))
        pstate.update(pb4_free=t_evv, stats_read=0, rstd_read=0,
                      xnT_users=0, sq_users=[0, 0], sq_tok=[0, 0])

        for pt in range(npt):
            T0 = pt * 512
            p.w("gpsimd", "tS", pstate["xnT_users"])
            p.w("gpsimd", "sS", max(pstate["sq_tok"]))
            p.w("gpsimd", "dT", pstate["vfm_tr"])
            t_xp = p.dma("gpsimd", "dW", b["xnT"][:], x_pre[:, :, T0:T0 + 512])
            p.w("tensor", "dW", t_xp)
            p.w("tensor", "vS", pstate["stats_read"])
            p.w("tensor", "sS", pstate["pb4_free"])
            for hc in range(16):
                p.mm(mm_(pb[4][0:1, 0:512], b["vones"][:], b["xnT"][:, hc, :],
                         hc == 0, hc == 15))
            t_sqmm = 0
            for half in range(2):
                p.w("scalar", "dW", t_xp)
                p.w("scalar", "tS", max(pstate["sq_users"]))
                p.w("scalar", "dT", pstate["vfm_tr"])
                t_sq = sop(act_(b["sqb"][:], b["xnT"][:, 8 * half:8 * half + 8],
                                AF.Square, b["zc"][:, 0:1]))
                pstate["sq_tok"][half] = t_sq
                p.w("tensor", "sS", t_sq)
                for hc in range(8):
                    t_sqmm = p.mm(mm_(pb[5][0:1, 0:512], b["vones"][:],
                                      b["sqb"][:, hc, :],
                                      half == 0 and hc == 0,
                                      half == 1 and hc == 7), tok=(hc == 7))
                pstate["sq_users"][half] = t_sqmm
            p.w("vector", "tS", t_sqmm)
            p.w("vector", "dT", pstate["vfm_tr"])
            vop(ts_(b["recips"][0:1, :], pb[4][0:1, :], 1.0 / 2048.0, ALU.mult))
            v_nm = vop(ts_(b["attnU"][0:1, :], b["recips"][0:1, :], -1.0, ALU.mult))
            vop(tt_(b["recips"][0:1, :], b["recips"][0:1, :], b["recips"][0:1, :],
                    ALU.mult))
            t_var = vop(stt_(b["recips"][0:1, :], pb[5][0:1, :], 1.0 / 2048.0,
                             b["recips"][0:1, :], ALU.mult, ALU.subtract))
            pstate["stats_read"] = t_var
            p.w("scalar", "vS", t_var)
            t_sd = sop(act_(b["recips"][32:33, :], b["recips"][0:1, :], AF.Sqrt,
                            b["epsc"][32:33, 0:1]))
            p.w("vector", "sS", t_sd)
            t_rst = vop(lambda e: e.reciprocal(b["recips"][0:1, :],
                                               b["recips"][32:33, :]))
            p.w("tensor", "vS", max(t_rst, pstate["rstd_read"]))
            p.mm(mm_(pb[3][:, 0:512], b["onesP"][:], b["recips"][0:1, :],
                     True, True))
            p.w("tensor", "sS", pstate["pb_free"][0])
            for hc in range(16):
                p.mm(mm_(pb[0][:, 0:512], b["wsl"][:, 0, 0, hc, :],
                         b["xnT"][:, hc, :], hc == 0, False))
            t_mm = p.mm(mm_(pb[0][:, 0:512], b["swk"][:], b["attnU"][0:1, :],
                            False, True), tok=True)
            p.w("scalar", "tS", t_mm)
            t_k = rope_evac(0, b["cosp"][:, T0:T0 + 512], b["sinp"][:, T0:T0 + 512],
                            b["Kg"][:, T0:T0 + 512], w=512, rstd_ps=pb[3])
            norms(b["Kg"][:, T0:T0 + 512], 32 + 30, t_k)
            p.w("tensor", "sS", pstate["pb_free"][1])
            for hc in range(16):
                p.mm(mm_(pb[1][:, 0:512], b["wsl"][:, 0, 1, hc, :],
                         b["xnT"][:, hc, :], hc == 0, False))
            t_mm2 = p.mm(mm_(pb[1][:, 0:512], b["swv"][:], b["attnU"][0:1, :],
                             False, True), tok=True)
            pstate["xnT_users"] = t_mm2
            p.w("scalar", "tS", t_mm2)
            p.w("scalar", "dT", pstate["vfm_tr"])
            t_vf = sop(lambda e: e.copy(b["vfm"][:, 0:512], pb[1][:, 0:512]))
            pstate["pb_free"][1] = t_vf
            p.w("vector", "sS", t_vf)
            t_vv = vop(tt_(b["vfm"][:, 0:512], b["vfm"][:, 0:512], pb[3][:, 0:512],
                           ALU.mult))
            pstate["rstd_read"] = t_vv
            p.w("sync", "vS", t_vv)
            p.w("sync", "dT", pstate["vfm_tr"])
            ci0 = T0 // 128
            pstate["vfm_tr"] = p.dmaT(
                RawAP(b["Vg"], ci0 * 128, [[VG_FLAT, 128], [128, 4], [1, 128]]),
                b["vfm"][:, 0:512])
        p.fence()
        ph1(f"i{it}_pre")

        if KPH < 3:
            p.fence()
            continue
        ph0(f"i{it}_span")
        wstate = {"tok": [0, 0], "used": [0, 0]}
        pstate["vfm_used"] = 0
        pstate["vcp"] = 0
        for ci in range(NSC):
            gop(lambda e, ci=ci: e.memset(
                RawAP(b["V"], ci * 2080 + 64, [[V_FLAT, 128], [65, 32], [1, 1]]), 1.0))

        wstate["stored"] = [0, 0]

        def wslab(wi, Wd, sl, buf, cached):
            p.w("gpsimd" if not cached else "sync", "tS", wstate["used"][buf])
            if cached:
                wstate["tok"][buf] = p.dma("sync", "dX", b["wsl"][:, buf],
                                           d["wb_hbm"][wi, sl])
                return wstate["tok"][buf], "dX"
            p.w("gpsimd", "dX", wstate["stored"][buf])
            wstate["tok"][buf] = p.dma("gpsimd", "dW", b["wsl"][:, buf], Wd[sl])
            return wstate["tok"][buf], "dW"

        sstate = {"norm_done": 0, "xnT_mm": 0, "sq_mm": 0, "sq_tok": 0,
                  "rows_read": 0}
        for tt in range(3):
            for bi in range(4):
                cosgen(tt * 4 + bi, b["cs_c"][:, bi * 128:(bi + 1) * 128],
                       b["cs_s"][:, bi * 128:(bi + 1) * 128])
            p.w("gpsimd", "tS", sstate["xnT_mm"])
            p.w("gpsimd", "vS", p.cnt["vS"])
            p.w("gpsimd", "sS", sstate["sq_tok"])
            p.w("gpsimd", "dT", pstate["vfm_tr"])
            t_xs = p.dma("gpsimd", "dW", b["xnT"][:],
                         d["x_spanT"][it][:, :, tt * 512:(tt + 1) * 512])
            p.w("tensor", "dW", t_xs)
            p.w("tensor", "vS", sstate["rows_read"])
            for hc in range(16):
                p.mm(mm_(pb[3][0:1, 0:512], b["vones"][:], b["xnT"][:, hc, :],
                         hc == 0, hc == 15))
            t_sqmm = 0
            for qt in range(8):
                p.w("scalar", "dW", t_xs)
                p.w("scalar", "tS", sstate["sq_mm"])
                p.w("scalar", "dT", pstate["vfm_tr"])
                t_sq = sop(act_(b["sqs"][:], b["xnT"][:, 2 * qt:2 * qt + 2],
                                AF.Square, b["zc"][:, 0:1]))
                sstate["sq_tok"] = t_sq
                p.w("tensor", "sS", t_sq)
                for hc in range(2):
                    t_sqmm = p.mm(mm_(pb[5][0:1, 0:512], b["vones"][:],
                                      b["sqs"][:, hc, :],
                                      qt == 0 and hc == 0,
                                      qt == 7 and hc == 1), tok=(hc == 1))
                sstate["sq_mm"] = t_sqmm
            p.w("vector", "tS", t_sqmm)
            p.w("vector", "dT", pstate["vfm_tr"])
            vop(ts_(b["rows"][0:1, :], pb[3][0:1, :], 1.0 / 2048.0, ALU.mult))
            vop(tt_(b["rows2"][0:1, :], b["rows"][0:1, :], b["rows"][0:1, :],
                    ALU.mult))
            t_var = vop(stt_(b["rows2"][0:1, :], pb[5][0:1, :], 1.0 / 2048.0,
                             b["rows2"][0:1, :], ALU.mult, ALU.subtract))
            p.w("scalar", "vS", t_var)
            t_sd = sop(act_(b["rows2"][32:33, :], b["rows2"][0:1, :], AF.Sqrt,
                            b["epsc"][32:33, 0:1]))
            p.w("vector", "sS", t_sd)
            t_rst = vop(lambda e: e.reciprocal(b["rows2"][0:1, :],
                                               b["rows2"][32:33, :]))
            sstate["rows_read"] = t_rst
            p.w("tensor", "vS", t_rst)
            p.mm(mm_(pb[3][:, 0:512], b["onesP"][:], b["rows"][0:1, :],
                     True, True))
            t_bc = p.mm(mm_(pb[5][:, 0:512], b["onesP"][:], b["rows2"][0:1, :],
                            True, True), tok=True)
            p.w("vector", "tS", t_bc)
            t_nrm = 0
            for hc in range(16):
                vop(tt_(b["xnT"][:, hc, :], b["xnT"][:, hc, :], pb[3][:, 0:512],
                        ALU.subtract))
                t_nrm = vop(tt_(b["xnT"][:, hc, :], b["xnT"][:, hc, :],
                                pb[5][:, 0:512], ALU.mult))
            sstate["norm_done"] = t_nrm
            last_mm = 0
            t0 = tt * 512
            wlist = [("K", d["WkT"]), ("V", d["WvT"])] + \
                    ([("Q", d["WqT"])] if tt >= 1 else [])

            def stage_B(g):
                pp = g["pp"]
                if g["wname"] == "V":
                    p.w("scalar", "tS", max(g["t_mm"], pstate["vfm_used"]))
                    g["t_ev"] = sop(lambda e: e.copy(b["vfm"][:, 0:512],
                                                     pb[pp][:, 0:512]))
                else:
                    kb = g["pc"] % 2
                    ku = b["kuB"] if kb else b["ku"]
                    p.w("scalar", "tS", g["t_mm"])
                    p.w("scalar", "vS", pstate["ku_read"][kb])
                    g["t_ev"] = sop(lambda e, ku=ku: e.copy(ku[:, 0:512],
                                                            pb[pp][:, 0:512]))
                pstate["pb_free"][pp] = g["t_ev"]

            def stage_C(g):
                if g is None:
                    return
                if g["wname"] == "V":
                    p.w("tensor", "sS", max(g["t_ev"], pstate["vcp"]))
                    for k in range(4):
                        t_tr = p.mm(mm_(pb[4][:, k * 128:(k + 1) * 128],
                                        b["vfm"][:, k * 128:(k + 1) * 128],
                                        b["ident_b"][:], True, True),
                                    tok=(k == 3))
                    pstate["vfm_used"] = t_tr
                    g["t_tr"] = t_tr
                else:
                    rp = g["pc"] % 2
                    ku = b["kuB"] if rp else b["ku"]
                    p.w("tensor", "sS", g["t_ev"])
                    p.w("tensor", "vS", pstate["rot_free"][rp])
                    g["t_rot"] = p.mm(mm_(pb[2 + rp][:, 0:512], b["prot_b"][:],
                                          ku[:, 0:512], True, True), tok=True)

            def stage_D(g):
                if g is None:
                    return
                if g["wname"] == "V":
                    p.w("scalar", "tS", g["t_tr"])
                    vdst = RawAP(b["V"], tt * 4 * 2080 + g["pc"] * 130,
                                 [[V_FLAT, 128], [2080, 4], [65, 2], [1, 64]])
                    pstate["vcp"] = sop(
                        lambda e, vdst=vdst: e.copy(vdst, pb[4][:, 0:512]))
                    return
                kb = g["pc"] % 2
                ku = b["kuB"] if kb else b["ku"]
                p.w("vector", "sS", g["t_ev"])
                pstate["ku_read"][kb] = vop(tt_(b["t1r"][:, 0:512], ku[:, 0:512],
                                                b["cs_c"][:], ALU.mult))
                p.w("vector", "tS", g["t_rot"])
                pstate["rot_free"][kb] = vop(tt_(b["t2r"][:, 0:512],
                                                 pb[2 + kb][:, 0:512],
                                                 b["cs_s"][:], ALU.mult))
                g["t_k"] = vop(tt_(g["dst"], b["t1r"][:, 0:512],
                                   b["t2r"][:, 0:512], ALU.add))
                sqn = b["sqnB"] if kb else b["sqn"]
                p.w("gpsimd", "tS", pstate["sqn_free"][kb])
                p.w("gpsimd", "vS", g["t_k"])
                g["t_sq"] = gop(tt_(sqn[:, 0:512], g["dst"], g["dst"], ALU.mult))

            def stage_E(g):
                if g is None or g["wname"] == "V":
                    return
                sq = g["pc"] % 2
                sqn = b["sqnB"] if sq else b["sqn"]
                p.w("tensor", "gS", g["t_sq"])
                p.w("tensor", "vS", pstate["nrm_free"])
                p.mm(mm_(pb[7][0:1, 0:512], b["eloC"][:], sqn[:, 0:512],
                         True, True))
                g["t_nm"] = p.mm(mm_(pb[6][0:1, 0:512], b["ehiC"][:],
                                     sqn[:, 0:512], True, True), tok=True)
                pstate["sqn_free"][sq] = g["t_nm"]

            def stage_F(g):
                if g is None or g["wname"] == "V":
                    return
                qcol0 = (32 + 2 * g["pc"]) if g["wname"] == "K" else 2 * g["pc"]
                p.w("vector", "tS", g["t_nm"])
                vop(red_(b["sc1"][0:1, 0:1], pb[7][0:1, 0:512], ALU.max))
                vop(tt_(b["nrm"][0:1, qcol0:qcol0 + 1],
                        b["nrm"][0:1, qcol0:qcol0 + 1],
                        b["sc1"][0:1, 0:1], ALU.max))
                vop(red_(b["sc1"][0:1, 0:1], pb[6][0:1, 0:512], ALU.max))
                pstate["nrm_free"] = vop(tt_(
                    b["nrm"][0:1, qcol0 + 1:qcol0 + 2],
                    b["nrm"][0:1, qcol0 + 1:qcol0 + 2],
                    b["sc1"][0:1, 0:1], ALU.max))

            gq = [None, None]
            for wname, Wd in wlist:
                wi = {"K": 0, "V": 1, "Q": 2}[wname]
                first_use = it == 0 and tt == (0 if wname != "Q" else 1)
                for sl in range(8):
                    buf = sl % 2
                    t_w, w_sem = wslab(wi, Wd, sl, buf, not first_use)
                    for q in range(2):
                        pc = 2 * sl + q
                        pp = pc % 2
                        p.w("tensor", "vS", sstate["norm_done"])
                        p.w("tensor", w_sem, t_w)
                        p.w("tensor", "sS", pstate["pb_free"][pp])
                        for hc in range(15):
                            p.mm(mm_(pb[pp][:, 0:512], b["wsl"][:, buf, q, hc, :],
                                     b["xnT"][:, hc, :], hc == 0, False))
                        t_mm = p.mm(mm_(
                            pb[pp][:, 0:512], b["wsl"][:, buf, q, 15, :],
                            b["xnT"][:, 15, :], False, True), tok=True)
                        last_mm = t_mm
                        wstate["used"][buf] = t_mm
                        if wname == "V":
                            dst = None
                        elif wname == "K":
                            dst = b["K"][:, pc, t0:t0 + 512]
                        else:
                            dst = b["Q"][:, pc, t0 - 512:t0]
                        g = {"wname": wname, "pc": pc, "pp": pp, "t_mm": t_mm,
                             "dst": dst}
                        stage_C(gq[0])
                        stage_E(gq[1])
                        stage_B(g)
                        stage_D(gq[0])
                        stage_F(gq[1])
                        gq = [g, gq[0]]
                    if first_use:
                        p.w("sync", "tS", wstate["used"][buf])
                        wstate["stored"][buf] = p.dma(
                            "sync", "dX", d["wb_hbm"][wi, sl], b["wsl"][:, buf])
            stage_C(gq[0])
            stage_E(gq[1])
            stage_D(gq[0])
            stage_F(gq[1])
            stage_E(gq[0])
            stage_F(gq[0])
            sstate["xnT_mm"] = last_mm
        p.fence()

        ph1(f"i{it}_span")
        if KPH < 4:
            p.fence()
            continue
        ph0(f"i{it}_bias")
        t_kvs = p.dma("sync", "dX", b["kvs"][:], d["kvn_span"][:].rearrange("a p c -> p a c"))
        t_kvp = p.dma("sync", "dX", b["kvp"][:, 0:npre], kvn_pre[:])
        t_db = p.dma("gpsimd", "dW", b["diag_b"][:], d["diag"][:])
        gop(lambda e: e.memset(b["eloR"][0:32, :], 0.0))
        gop(lambda e: e.memset(b["eloR"][0:1, 0:64], 1.0))
        gop(lambda e: e.memset(b["eloR"][32:33, 0:64], 0.0))
        gop(lambda e: e.memset(b["eloR"][32:33, 64:128], 1.0))
        gop(lambda e: e.memset(b["recips"][0:32, :], 0.0))
        gop(lambda e: e.memset(b["onesr"][0:1, :], 1.0))
        t_g = gop(lambda e: e.memset(b["onesb"][0:1, :], 1.0))
        p.w("vector", "dX", t_kvp)
        p.w("vector", "gS", t_g)
        bias_evac = 0
        for h in range(32):
            vop(tt_(b["sc1"][0:1, 0:1], b["nrm"][0:1, h:h + 1],
                    b["nrm"][0:1, 32 + h:33 + h], ALU.mult))
            t_v = p.cnt["vS"]
            p.w("scalar", "vS", t_v)
            t_s = sop(act_(b["sc1"][0:1, 1:2], b["sc1"][0:1, 0:1], AF.Sqrt,
                           b["zc"][0:1, 0:1]))
            p.w("vector", "sS", t_s)
            t_v = vop(ts_(b["sc1"][0:1, 2:3], b["sc1"][0:1, 1:2], -0.125, ALU.mult))
            p.w("tensor", "vS", t_v)
            p.w("tensor", "sS", bias_evac)
            t_mm = p.mm(mm_(pb[7][:, 0:1], b["onesr"][:], b["sc1"][0:1, 2:3],
                            True, True), tok=True)
            p.w("scalar", "tS", t_mm)
            bias_evac = sop(lambda e: e.copy(b["stats"][:, 9:10], pb[7][:, 0:1]))
            p.w("vector", "sS", bias_evac)
            vop(ts_(b["bias_s"][:, h, :], b["kvs"][:, it, :], b["stats"][:, 9:10],
                    ALU.add))
            if h >= 30:
                vop(ts_(b["bias_p"][:, h - 30, 0:npre], b["kvp"][:, 0:npre],
                        b["stats"][:, 9:10], ALU.add))
        if it == 0:
            wf_free = 0
            for ic in range(16):
                p.w("sync", "vS", wf_free)
                t_wf = p.dma("sync", "dX", b["wfb"][:], d["WfT"][ic])
                p.w("vector", "dX", t_wf)
                wf_free = vop(red_(b["stats"][:, 16 + ic:17 + ic], b["wfb"][:],
                                   ALU.add, absval=True))
            vop(red_(b["stats"][:, 10:11], b["stats"][:, 16:32], ALU.add))
            t_v = vop(lambda e: e.memset(b["stats"][:, 14:15], 1.0))
            p.w("tensor", "vS", t_v)
            p.w("tensor", "sS", bias_evac)
            t_mm = p.mm(mm_(pb[7][0:1, 0:1], b["stats"][:, 10:11],
                            b["stats"][:, 14:15], True, True), tok=True)
            p.w("scalar", "tS", t_mm)
            t_s = sop(lambda e: e.copy(b["sc1"][0:1, 8:9], pb[7][0:1, 0:1]))
            p.w("tensor", "sS", t_s)
            t_mm = p.mm(mm_(pb[7][:, 4:5], b["onesr"][:], b["sc1"][0:1, 8:9],
                            True, True), tok=True)
            p.w("vector", "tS", t_mm)
            vop(lambda e: e.tensor_copy(b["stats"][:, 11:12], pb[7][:, 4:5]))
            vop(ts_(b["perm"][:, 1:2], b["stats"][:, 11:12],
                    1.0 / (2048.0 * 2048.0), ALU.mult))
            vop(ts_(b["perm"][:, 0:1], b["perm"][:, 1:2], 1e-8, ALU.max))
            vop(lambda e: e.reciprocal(b["perm"][:, 2:3], b["perm"][:, 0:1]))
        p.fence()

        ph1(f"i{it}_bias")
        if KPH < 5:
            p.fence()
            continue
        ph0(f"i{it}_attn")
        exp_gate = [0, 0]
        av_gate = [0, 0, 0, 0]
        diag_gate = [0, 0, 0, 0]
        pb45_free = 0
        fact_free = 0
        sums_free = 0
        sums_evac = 0
        for qg in range(2):
            for pc in range(16):
                glob = pc == 15
                chunks = ([("pre", cp) for cp in range(npre)] if glob else []) + \
                         [("span", ci) for ci in range(min(NSC, 8 + 4 * qg))]
                meta = []
                for idx, (kind, ci) in enumerate(chunks):
                    off = 128 * max(0, ci - 4 - 4 * qg) if kind == "span" else 0
                    meta.append((kind, ci, idx % 2, off))

                def st_(kind, ci, par, off, pc=pc, qg=qg):
                    p.w("tensor", "sS", exp_gate[par])
                    t = 0
                    for h01 in range(2):
                        if kind == "span":
                            kl = b["K"][64 * h01:64 * h01 + 64, pc,
                                        ci * 128:ci * 128 + 128]
                        else:
                            kl = b["Kg"][64 * h01:64 * h01 + 64,
                                         ci * 128:ci * 128 + 128]
                        qr = b["Q"][64 * h01:64 * h01 + 64, pc,
                                    qg * 512 + off:qg * 512 + 512]
                        t = p.mm(mm_(pb[2 * h01 + par][:, off:512], kl, qr,
                                     True, True), tok=(h01 == 1))
                    return t

                st_tok = [0] * len(chunks)
                st_tok[0] = st_(*meta[0])
                for idx, (kind, ci, par, off) in enumerate(meta):
                    first, last = idx == 0, idx == len(chunks) - 1
                    p.w("scalar", "tS", max(st_tok[idx], av_gate[2 * 0 + par],
                                            av_gate[2 * 1 + par]))
                    p.w("scalar", "vS", max(diag_gate[2 * 0 + par],
                                            diag_gate[2 * 1 + par]))
                    t_exp = 0
                    for h01 in range(2):
                        h = 2 * pc + h01
                        bias = (b["bias_s"][:, h, ci:ci + 1] if kind == "span"
                                else b["bias_p"][:, h01, ci:ci + 1])
                        t_exp = sop(act_(b["Ab"][:, 2 * par + h01, off:512],
                                         pb[2 * h01 + par][:, off:512],
                                         AF.Exp, bias, 0.125))
                        exp_gate[par] = t_exp
                    if idx + 1 < len(chunks):
                        st_tok[idx + 1] = st_(*meta[idx + 1])
                    t_diag = 0
                    if kind == "span":
                        dq = ci - 4 - 4 * qg
                        if 0 <= dq < 4:
                            p.w("vector", "sS", t_exp)
                            for h01 in range(2):
                                t_diag = vop(tt_(
                                    b["Ab"][:, 2 * par + h01, dq * 128:dq * 128 + 128],
                                    b["Ab"][:, 2 * par + h01, dq * 128:dq * 128 + 128],
                                    b["diag_b"][:], ALU.mult))
                                diag_gate[2 * h01 + par] = t_diag
                    p.w("tensor", "sS", t_exp)
                    if t_diag:
                        p.w("tensor", "vS", t_diag)
                    if first:
                        p.w("tensor", "vS", pb45_free)
                        p.w("tensor", "sS", sums_evac)
                    for h01 in range(2):
                        if kind == "span":
                            o = ci * 2080 + (2 * pc + h01) * 65
                            vl = RawAP(b["V"], o, [[V_FLAT, 128], [1, 65]])
                            t_av = p.mm(mm_(pb[4 + h01][0:65, off:512], vl,
                                            b["Ab"][:, 2 * par + h01, off:512],
                                            first, last), tok=True)
                        else:
                            o = ci * 128 + 64 * h01
                            vl = RawAP(b["Vg"], o, [[VG_FLAT, 128], [1, 64]])
                            p.mm(mm_(pb[4 + h01][0:64, off:512], vl,
                                     b["Ab"][:, 2 * par + h01, off:512],
                                     first, last))
                            t_av = p.mm(mm_(pb[4 + h01][64:65, off:512], b["vones"][:],
                                            b["Ab"][:, 2 * par + h01, off:512],
                                            first, last), tok=True)
                        av_gate[2 * h01 + par] = t_av
                t_last = av_gate[2 * 1 + (len(chunks) - 1) % 2]
                p.w("scalar", "tS", max(t_last, sums_free))
                sop(lambda e: e.copy(b["recips"][0:1, :], pb[4][64:65, :]))
                sums_evac = sop(lambda e: e.copy(b["recips"][32:33, :],
                                                 pb[5][64:65, :]))
                p.w("tensor", "sS", sums_evac)
                p.w("tensor", "vS", fact_free)
                sums_free = p.mm(mm_(pb[6][:, :], b["eloR"][:], b["recips"][:, :],
                                     True, True), tok=True)
                p.w("vector", "tS", sums_free)

                def _recip(e):
                    with p.nc.allow_low_precision(reason="fact is bf16 by design"):
                        return e.reciprocal(b["fact"][:], pb[6][:, :])
                fact_free = vop(_recip)
                vop(lambda e: e.tensor_copy(b["attnU"][0:64, :], pb[4][0:64, :]))
                pb45_free = vop(lambda e: e.tensor_copy(b["attnU"][64:128, :],
                                                        pb[5][0:64, :]))
                vop(tt_(b["attnT"][:, pc, qg * 512:qg * 512 + 512],
                        b["attnU"][:], b["fact"][:], ALU.mult))
        p.fence()

        ph1(f"i{it}_attn")
        if KPH < 6:
            p.fence()
            continue
        ph0(f"i{it}_wo")
        t_wo_w = p.dma("gpsimd", "dW", b["wbig"][:],
                       d["WoT"][:].rearrange("a p b -> p a b"))
        t_bo = p.dma("gpsimd", "dW", b["bo_sb"][:], d["bo"][:])
        pb_free_wo = 0
        out1_free = (0, 0, 0)
        sq_prev = 0
        for t8 in range(8):
            p.w("sync", "vS", out1_free[0])
            p.w("sync", "sS", out1_free[1])
            p.w("sync", "dX", out1_free[2])
            t_x = p.dma("sync", "dX", b["out1"][:],
                        d["x_span"][it, 512 + t8 * 128:512 + (t8 + 1) * 128, :])
            p.w("tensor", "dW", max(t_wo_w, t_bo))
            p.w("tensor", "vS", pb_free_wo)
            for n in range(4):
                p.mm(mm_(pb[n][:, :], b["onesb"][0:1, :],
                         b["bo_sb"][0:1, n * 512:(n + 1) * 512], True, False))
            t_wo = 0
            for pc in range(16):
                for n in range(4):
                    t_wo = p.mm(mm_(pb[n][:, :], b["attnT"][:, pc, t8 * 128:(t8 + 1) * 128],
                                    b["wbig"][:, pc, n * 512:(n + 1) * 512],
                                    False, pc == 15),
                                tok=(pc == 15 and n == 3))
            p.w("vector", "tS", t_wo)
            p.w("vector", "dX", t_x)
            for n in range(4):
                t_ev = vop(tt_(b["out1"][:, n * 512:(n + 1) * 512],
                               b["out1"][:, n * 512:(n + 1) * 512], pb[n][:, :],
                               ALU.add))
            pb_free_wo = t_ev
            vop(red_(b["stats"][:, 0:1], b["out1"][:], ALU.add))
            vop(red_(b["stats"][:, 12:13], b["out1"][:], ALU.max))
            t_v = vop(red_(b["stats"][:, 13:14], b["out1"][:], ALU.min))
            p.w("scalar", "vS", t_v)
            p.w("scalar", "vS", sq_prev)
            t_sq = sop(act_(b["fin"][:], b["out1"][:], AF.Square, b["zc"][:, 0:1],
                            1.0, b["stats"][:, 1:2]))
            out1_sq = t_sq
            p.w("vector", "sS", t_sq)
            vop(ts_(b["stats"][:, 2:3], b["stats"][:, 0:1], 1.0 / 2048.0, ALU.mult))
            vop(ts_(b["stats"][:, 3:4], b["stats"][:, 1:2], 1.0 / 2048.0, ALU.mult))
            vop(tt_(b["stats"][:, 4:5], b["stats"][:, 2:3], b["stats"][:, 2:3], ALU.mult))
            t_v = vop(tt_(b["stats"][:, 5:6], b["stats"][:, 3:4], b["stats"][:, 4:5],
                          ALU.subtract))
            p.w("scalar", "vS", t_v)
            t_s = sop(act_(b["stats"][:, 6:7], b["stats"][:, 5:6], AF.Sqrt,
                           b["epsc"][:, 0:1]))
            p.w("vector", "sS", t_s)
            vop(lambda e: e.reciprocal(b["stats"][:, 7:8], b["stats"][:, 6:7]))
            vop(tt_(b["stats"][:, 14:15], b["stats"][:, 12:13], b["stats"][:, 2:3],
                    ALU.subtract))
            vop(tt_(b["stats"][:, 15:16], b["stats"][:, 2:3], b["stats"][:, 13:14],
                    ALU.subtract))
            vop(tt_(b["stats"][:, 14:15], b["stats"][:, 14:15], b["stats"][:, 15:16],
                    ALU.max))
            vop(tt_(b["stats"][:, 15:16], b["stats"][:, 14:15], b["stats"][:, 7:8],
                    ALU.mult))
            vop(ts_(b["stats"][:, 9:10], b["stats"][:, 15:16], 1.0 / 127.0, ALU.mult,
                    1e-8, ALU.max))
            vop(tt_(b["gam"][:, t8:t8 + 1], b["stats"][:, 9:10], b["perm"][:, 0:1],
                    ALU.mult))
            vop(lambda e: e.reciprocal(b["stats"][:, 10:11], b["stats"][:, 9:10]))
            vop(tt_(b["stats"][:, 11:12], b["stats"][:, 7:8], b["stats"][:, 10:11],
                    ALU.mult))
            vop(stt_(b["stats"][:, 8:9], b["stats"][:, 2:3], -1.0, b["stats"][:, 11:12],
                     ALU.mult, ALU.mult))
            vop(ts_(b["stats"][:, 8:9], b["stats"][:, 8:9], 128.0, ALU.add))
            t_w6 = vop(ts_(b["w6"][:], b["out1"][:], b["stats"][:, 11:12],
                           ALU.mult, b["stats"][:, 8:9], ALU.add))
            sq_prev = ladder_v(b["w6"][:], b["r6"][:], b["fin"][:], 8)
            vop(ts_(b["xq"][:, t8, :], b["r6"][:], -128.0, ALU.add))
            p.w("sync", "vS", pb_free_wo)
            t_st = p.dma("sync", "dX", d["out1_scr"][t8 * 128:(t8 + 1) * 128, :],
                         b["out1"][:])
            out1_free = (t_w6, out1_sq, t_st)
        p.fence()

        ph1(f"i{it}_wo")
        if KPH < 7:
            p.fence()
            continue
        ph0(f"i{it}_ffn")
        if it == 0:
            wf2_free = [0, 0]
            for ic in range(16):
                bi = ic % 2
                p.w("sync", "vS", wf2_free[bi])
                t_wf = p.dma("sync", "dX", b["wf2"][:, bi, :], d["WfT"][ic])
                p.w("vector", "dX", t_wf)
                vop(ts_(b["out1r"][:, 0, :], b["wf2"][:, bi, :], b["perm"][:, 2:3],
                        ALU.mult, 0.5, ALU.is_gt))
                vop(ts_(b["out1r"][:, 1, :], b["wf2"][:, bi, :], b["perm"][:, 2:3],
                        ALU.mult, -0.5, ALU.is_lt))
                wf2_free[bi] = vop(tt_(b["wbig"][:, ic, :], b["out1r"][:, 0, :],
                                       b["out1r"][:, 1, :], ALU.subtract))
            wq_ready_v = p.cnt["vS"]
            p.w("sync", "vS", wq_ready_v)
            t_wq = 0
            p.dma("sync", "dX", d["wq_hbm"][:], b["wbig"][:])
        else:
            wq_ready_v = 0
            t_wq = p.dma("sync", "dX", b["wbig"][:], d["wq_hbm"][:])
        tr_list = []
        for t8 in range(8):
            tr_list.append(p.dmaT(b["xqT"][:, :, t8 * 128:(t8 + 1) * 128],
                                  b["xq"][:, t8, :]))
        fin_last = 0
        o1_free = [0, 0]
        st_free = [0, 0]
        for t8 in range(8):
            bi = t8 % 2
            p.w("sync", "vS", o1_free[bi])
            t_o1 = p.dma("sync", "dX", b["out1r"][:, bi, :],
                         d["out1_scr"][t8 * 128:(t8 + 1) * 128, :])
            p.w("tensor", "dT", tr_list[t8])
            p.w("tensor", "dX", t_wq)
            p.w("tensor", "vS", wq_ready_v)
            p.w("tensor", "vS", fin_last)
            t_f = 0
            for ic in range(16):
                for n in range(4):
                    t_f = p.mm(mm_(pb[n][:, :], b["xqT"][:, ic, t8 * 128:(t8 + 1) * 128],
                                   b["wbig"][:, ic, n * 512:(n + 1) * 512],
                                   ic == 0, ic == 15),
                               tok=(ic == 15 and n == 3))
            p.w("vector", "tS", t_f)
            p.w("vector", "dX", t_o1)
            p.w("vector", "dT", tr_list[-1])
            p.w("vector", "dO", st_free[bi])
            for n in range(4):
                t_fin = vop(stt_(b["fin2"][:, bi, n * 512:(n + 1) * 512], pb[n][:, :],
                                 b["gam"][:, t8:t8 + 1],
                                 b["out1r"][:, bi, n * 512:(n + 1) * 512],
                                 ALU.mult, ALU.add))
            fin_last = t_fin
            o1_free[bi] = t_fin
            p.w("sync", "vS", t_fin)
            st_free[bi] = p.dma("sync", "dO", d["out"][it, t8 * 128:(t8 + 1) * 128, :],
                                b["fin2"][:, bi, :])
        p.fence()
        ph1(f"i{it}_ffn")

    p.w("sync", "dO", p.cnt["dO"])



def kernel(**inputs):
    in_maps = stage_inputs(inputs)
    ktabs = build_program(None)
    nc, es = build_program(ktabs)
    try:
        res = run_bass_kernel_spmd(nc, in_maps, core_ids=list(range(8)))
    finally:
        es.close()
    return gather_output(res.results)
